# version 29 (final)
# baseline (speedup 1.0000x reference)
import numpy as np
import ml_dtypes
import concourse.bass as bass
import concourse.mybir as mybir
from concourse.bass_utils import run_bass_kernel_spmd
from concourse.alu_op_type import AluOpType

F32 = mybir.dt.float32
BF16 = mybir.dt.bfloat16
AF = mybir.ActivationFunctionType
AX = mybir.AxisListType.X
bfloat16 = ml_dtypes.bfloat16
MULT = AluOpType.mult
ADD = AluOpType.add
SUB = AluOpType.subtract
MAXOP = AluOpType.max
MINOP = AluOpType.min
ISGT = AluOpType.is_gt

D = 1024
T = 2048
H = 16
DK = 64
FF = 4096
NCH = 8
NTC = 16
VROW = H * 65
NSUP = 2
SUPT = 1024
DEBUG = False


class Ctr:
    def __init__(self, nc, name, step=1):
        self._cm = nc.semaphore(name)
        self.h = self._cm.__enter__()
        self.step = step
        self.v = 0


class Rec:
    def __init__(self):
        self.reset()

    def reset(self):
        self.ops = {"sync": [], "tensor": [], "scalar": [], "vector": [], "gpsimd": []}
        self.waited = {k: {} for k in self.ops}

    def op(self, eng, fn, waits=(), inc=None):
        w = []
        for (c, v) in waits:
            if v is None or v <= 0:
                continue
            key = id(c)
            if self.waited[eng].get(key, -1) >= v:
                continue
            self.waited[eng][key] = v
            w.append((c.h, v))
        if inc is not None:
            inc.v += inc.step

        def replay(engine, fn=fn, w=tuple(w), inc=inc):
            for (h, v) in w:
                engine.wait_ge(h, v)
            inst = fn(engine)
            if inc is not None:
                inst.then_inc(inc.h, inc.step)
        self.ops[eng].append(replay)
        return inc.v if inc is not None else None

    def flush(self, nc):
        if not any(self.ops.values()):
            return
        with nc.Block() as block:
            for name in ("sync", "tensor", "scalar", "vector", "gpsimd"):
                lst = self.ops[name]
                if lst:
                    def body(engine, lst=tuple(lst)):
                        for f in lst:
                            f(engine)
                    getattr(block, name)(body)
        self.reset()


def tap(t, part0, nparts, off, dims):
    W = int(t.shape[1])
    return bass.AP(t, part0 * W + off, [[W, nparts]] + [list(d) for d in dims])


def bcast_row(t, row, off, w, nrep):
    W = int(t.shape[1])
    return bass.AP(t, row * W + off, [[W, 1], [0, nrep], [1, w]])


def build_nc():
    nc = bass.Bass("TRN2", target_bir_lowering=False)

    x_h = nc.dram_tensor("x", [T, D], F32, kind="ExternalInput")
    wq_h = nc.dram_tensor("wq", [D, D], F32, kind="ExternalInput")
    wk_h = nc.dram_tensor("wk", [D, D], F32, kind="ExternalInput")
    wv_h = nc.dram_tensor("wv", [128, 8 * D], F32, kind="ExternalInput")
    wo_h = nc.dram_tensor("wo", [128, 8 * D], F32, kind="ExternalInput")
    wfc_h = nc.dram_tensor("wfc", [FF, D], F32, kind="ExternalInput")
    wpj_h = nc.dram_tensor("wpj", [FF, D], F32, kind="ExternalInput")
    bq_h = nc.dram_tensor("bqv", [128, 8], F32, kind="ExternalInput")
    bk_h = nc.dram_tensor("bkv", [128, 8], F32, kind="ExternalInput")
    bv_h = nc.dram_tensor("bvv", [1, D], F32, kind="ExternalInput")
    bo_h = nc.dram_tensor("bov", [1, D], F32, kind="ExternalInput")
    cosf_h = nc.dram_tensor("cosf", [128, T], F32, kind="ExternalInput")
    sinf_h = nc.dram_tensor("sinf", [128, T], F32, kind="ExternalInput")
    cb_h = nc.dram_tensor("cb", [128, 4 * 128], BF16, kind="ExternalInput")
    onesrow_h = nc.dram_tensor("onesrow", [1, 128], BF16, kind="ExternalInput")
    cvals_h = nc.dram_tensor("cvals_in", [128, 8], F32, kind="ExternalInput")
    out_h = nc.dram_tensor("out", [T, D], F32, kind="ExternalOutput")
    ascr_h = nc.dram_tensor("ascr", [T, D], BF16,
                            kind="ExternalOutput" if DEBUG else "Internal")
    if DEBUG:
        dbg = {
            "dxnT": nc.dram_tensor("dxnT", [128, NCH * T], BF16, kind="ExternalOutput"),
            "dvsb": nc.dram_tensor("dvsb", [128, NTC * VROW], BF16, kind="ExternalOutput"),
            "dqr0": nc.dram_tensor("dqr0", [128, 4 * T], BF16, kind="ExternalOutput"),
            "dkr0": nc.dram_tensor("dkr0", [128, 4 * T], BF16, kind="ExternalOutput"),
            "dqr1": nc.dram_tensor("dqr1", [128, 4 * T], BF16, kind="ExternalOutput"),
            "dkr1": nc.dram_tensor("dkr1", [128, 4 * T], BF16, kind="ExternalOutput"),
            "dcrep": nc.dram_tensor("dcrep", [128, 64], F32, kind="ExternalOutput"),
            "dLbuf": nc.dram_tensor("dLbuf", [64, 512], F32, kind="ExternalOutput"),
            "dRbuf": nc.dram_tensor("dRbuf", [64, 512], F32, kind="ExternalOutput"),
            "dosb": nc.dram_tensor("dosb", [128, NCH * T], BF16, kind="ExternalOutput"),
            "dhsb": nc.dram_tensor("dhsb", [128, 32 * SUPT], BF16, kind="ExternalOutput"),
            "dqn": nc.dram_tensor("dqn", [16, T], BF16, kind="ExternalOutput"),
            "dkn": nc.dram_tensor("dkn", [16, T], BF16, kind="ExternalOutput"),
            "dmvh": nc.dram_tensor("dmvh", [128, 64], F32, kind="ExternalOutput"),
            "dosbr": nc.dram_tensor("dosbr", [128, NCH * T], BF16, kind="ExternalOutput"),
            "dcosb0": nc.dram_tensor("dcosb0", [128, T], BF16, kind="ExternalOutput"),
            "dcosb1": nc.dram_tensor("dcosb1", [128, T], BF16, kind="ExternalOutput"),
            "drr0": nc.dram_tensor("drr0", [128, T], BF16, kind="ExternalOutput"),
            "drr1": nc.dram_tensor("drr1", [128, T], BF16, kind="ExternalOutput"),
            "dpool": nc.dram_tensor("dpool", [128, 32], F32, kind="ExternalOutput"),
            "dmv": nc.dram_tensor("dmv", [128, 32], F32, kind="ExternalOutput"),
            "dsqv": nc.dram_tensor("dsqv", [128, 16], F32, kind="ExternalOutput"),
            "dmsum": nc.dram_tensor("dmsum", [128, 16], F32, kind="ExternalOutput"),
            "drstd": nc.dram_tensor("drstd", [128, 16], F32, kind="ExternalOutput"),
            "dnegms": nc.dram_tensor("dnegms", [128, 16], F32, kind="ExternalOutput"),
        }

    dM = Ctr(nc, "dM", 16)
    dG = Ctr(nc, "dG", 16)
    dXb = [Ctr(nc, "dX0", 16), Ctr(nc, "dX1", 16)]
    dATb = [Ctr(nc, "dAT0", 16), Ctr(nc, "dAT1", 16)]
    dASb = [Ctr(nc, "dAS%d" % i_, 16) for i_ in range(6)]
    dLb = [Ctr(nc, "dL0", 16), Ctr(nc, "dL1", 16)]
    dOSTb = [Ctr(nc, "dOST0", 16), Ctr(nc, "dOST1", 16)]
    dWQb = [Ctr(nc, "dWQ0", 16), Ctr(nc, "dWQ1", 16)]
    dWFb = [Ctr(nc, "dWF0", 16), Ctr(nc, "dWF1", 16)]
    dRRb = [Ctr(nc, "dRR0", 16), Ctr(nc, "dRR1", 16)]
    dWBb = [Ctr(nc, "dWB0", 16), Ctr(nc, "dWB1", 16)]
    dOUb = [Ctr(nc, "dOU0", 16), Ctr(nc, "dOU1", 16)]
    cT = Ctr(nc, "cT")
    cS = Ctr(nc, "cS")
    cV = Ctr(nc, "cV")

    rec = Rec()
    OP = rec.op

    def alloc(name, shape, dtype, side=None):
        if side is None:
            cm = nc.sbuf_tensor(name, shape, dtype)
        else:
            cm = nc.sbuf_tensor(name, shape, dtype, side=side)
        return cm, cm.__enter__()

    P = {}
    cms = []
    for name, shape, dt_ in [
        ("xnT", [128, NCH * T], BF16),
        ("xt0", [128, D], F32), ("xt1", [128, D], F32),
        ("xn0", [128, D], BF16), ("xn1", [128, D], BF16),
        ("bnout", [128, 12], F32), ("mv", [128, 32], F32),
        ("mvh", [128, 64], F32), ("msum", [128, 16], F32),
        ("pool", [128, 32], F32), ("rstdf", [128, 16], F32),
        ("sqv", [128, 16], F32), ("rstd", [128, 16], F32), ("negms", [128, 16], F32),
        ("bq_sb", [128, 8], F32), ("bk_sb", [128, 8], F32),
        ("bvb", [1, D], BF16), ("bob", [1, D], BF16),
        ("cbc", [128, 4 * 128], BF16), ("onesr", [1, 128], BF16),
        ("Lbuf", [64, 512], F32), ("Rbuf", [64, 512], F32),
        ("crep", [128, 64], F32), ("negcf", [1, 64], F32),
        ("five", [128, 1], F32), ("fifth", [128, 1], F32),
        ("cvals", [128, 8], F32),
    ]:
        cm, t = alloc(name, shape, dt_)
        cms.append(cm)
        P[name] = t
    xnT = P["xnT"]
    xts = [P["xt0"], P["xt1"]]
    xns = [P["xn0"], P["xn1"]]
    cbc = P["cbc"]

    def ident():
        return cbc[:, 0:128]

    def sperm():
        return cbc[:, 128:256]

    def trimask():
        return cbc[:, 256:384]

    def onesm(c):
        return cbc[:, 384 + c * 16:384 + (c + 1) * 16]

    CVALS = [1e-5, 2e-5, -1.0, -0.1, 0.125, 0.0, 1.0, 5.0]
    for i, val in enumerate(CVALS):
        nc.const_aps.register(F32, val, P["cvals"][:, i:i + 1])

    osb24_cm, osb24 = alloc("osb24", [128, T], BF16)
    osb25_cm, osb25 = alloc("osb25", [128, T], BF16)
    osb_cm, osb = alloc("osb", [128, NCH * T], BF16)
    wot_cm, wot = alloc("wot", [128, 8 * D], BF16)
    vsb_cm, vsb = alloc("vsb", [128, NTC * VROW], BF16)
    wvt_cm, wvt = alloc("wvt", [128, 8 * D], BF16, side="right")

    OP("sync", lambda e: e.dma_start(P["bq_sb"][:, :], bq_h.ap()), inc=dM)
    OP("sync", lambda e: e.dma_start(P["bk_sb"][:, :], bk_h.ap()), inc=dM)
    OP("sync", lambda e: e.dma_start(cbc[:, :], cb_h.ap()), inc=dM)
    cl0 = OP("sync", lambda e: e.dma_start(P["onesr"][:, :], onesrow_h.ap()), inc=dM)
    cvld = OP("sync", lambda e: e.dma_start(P["cvals"][:, :], cvals_h.ap()), inc=dM)
    OP("gpsimd", lambda e: e.dma_start(P["bvb"][:, :], bv_h.ap()), inc=dG)
    OP("gpsimd", lambda e: e.dma_start(P["bob"][:, :], bo_h.ap()), inc=dG)
    wvt_ld = OP("gpsimd", lambda e: e.dma_start(wvt[:, :], wv_h.ap()), inc=dG)
    wot_ld = OP("gpsimd", lambda e: e.dma_start(wot[:, :], wo_h.ap()), inc=dG)

    with nc.psum_tensor("psT0", [128, 512], F32) as psT0, \
         nc.psum_tensor("psT1", [128, 512], F32) as psT1, \
         nc.psum_tensor("psV0", [128, 1024], F32) as psV0, \
         nc.psum_tensor("psV1", [128, 1024], F32) as psV1:
        psT = [psT0, psT1]
        psV = [psV0, psV1]
        ones_ap = bass.AP(vsb, 64, [[NTC * VROW, 128], [VROW, NTC], [65, H], [1, 1]])
        OP("vector", lambda e: e.memset(P["five"][:, :], 5.0))
        OP("vector", lambda e: e.memset(ones_ap, 1.0))
        evacT = {0: None, 1: None}
        evacV = {0: None, 1: None}
        xn_done = [None] * NTC
        sqd_l = [None] * NTC

        def stage_A(t):
            w = [(cS, xn_done[t - 2])] if t >= 2 else []
            ld = OP("sync", lambda e, t=t: e.dma_start(
                xts[t % 2][:, :], x_h.ap()[t * 128:(t + 1) * 128, :]), waits=w,
                inc=dXb[t % 2])
            OP("vector", lambda e, t=t: e.bn_stats(
                P["bnout"][:, 0:6], xts[t % 2][:, 0:512]), waits=[(dXb[t % 2], ld)])
            OP("vector", lambda e, t=t: e.bn_stats(
                P["bnout"][:, 6:12], xts[t % 2][:, 512:1024]))
            OP("vector", lambda e, t=t: e.bn_aggr(
                P["mvh"][:, 4 * t:4 * t + 2], P["bnout"][:, 0:6]))
            ag2 = OP("vector", lambda e, t=t: e.bn_aggr(
                P["mvh"][:, 4 * t + 2:4 * t + 4], P["bnout"][:, 6:12]), inc=cV)
            mc = 4 * t
            md = OP("vector", lambda e, mc=mc, t=t: e.tensor_tensor(
                P["mv"][:, t:t + 1], P["mvh"][:, mc:mc + 1],
                P["mvh"][:, mc + 2:mc + 3], SUB), waits=[(cV, ag2)], inc=cV)
            OP("vector", lambda e, mc=mc, t=t: e.tensor_tensor(
                P["msum"][:, t:t + 1], P["mvh"][:, mc:mc + 1],
                P["mvh"][:, mc + 2:mc + 3], ADD))
            msq = OP("vector", lambda e, t=t: e.tensor_tensor(
                P["pool"][:, t:t + 1], P["mv"][:, t:t + 1], P["mv"][:, t:t + 1],
                MULT), waits=[(cV, md)], inc=cV)
            cd = OP("vector", lambda e, mc=mc, t=t: e.scalar_tensor_tensor(
                P["pool"][:, 16 + t:17 + t], P["pool"][:, t:t + 1], 0.5,
                P["mvh"][:, mc + 1:mc + 2], MULT, ADD), waits=[(cV, msq)], inc=cV)
            vd = OP("vector", lambda e, mc=mc, t=t: e.tensor_tensor(
                P["mv"][:, 16 + t:17 + t], P["pool"][:, 16 + t:17 + t],
                P["mvh"][:, mc + 3:mc + 4], ADD), waits=[(cV, cd)], inc=cV)
            sqd_l[t] = OP("scalar", lambda e, t=t: e.activation(
                P["sqv"][:, t:t + 1], P["mv"][:, 16 + t:17 + t], AF.Sqrt,
                bias=2e-5), waits=[(cV, vd), (dM, cvld)], inc=cS)

        def stage_B(t):
            rr = OP("vector", lambda e, t=t: e.reciprocal(
                P["rstd"][:, t:t + 1], P["sqv"][:, t:t + 1]),
                waits=[(cS, sqd_l[t])], inc=cV)
            OP("vector", lambda e, t=t: e.scalar_tensor_tensor(
                P["negms"][:, t:t + 1], P["msum"][:, t:t + 1],
                -0.7071067811865476,
                P["rstd"][:, t:t + 1], MULT, MULT), waits=[(cV, rr)])
            sf = OP("vector", lambda e, t=t: e.tensor_single_scalar(
                P["rstdf"][:, t:t + 1], P["rstd"][:, t:t + 1],
                1.4142135623730951, MULT), inc=cV)
            xn_done[t] = OP("scalar", lambda e, t=t: e.activation(
                xns[t % 2][:, :], xts[t % 2][:, :], AF.Identity,
                bias=P["negms"][:, t:t + 1], scale=P["rstdf"][:, t:t + 1]),
                waits=[(cV, sf)], inc=cS)

        def stage_TP(t):
            for cg in range(2):
                bank = (t * 2 + cg) % 2
                mmw = [(cS, xn_done[t]), (cS, evacT[bank])]
                last = None
                for c4 in range(4):
                    c = cg * 4 + c4
                    last = OP("tensor", lambda e, t=t, c=c, c4=c4, bank=bank: e.matmul(
                        psT[bank][:, c4 * 128:(c4 + 1) * 128],
                        xns[t % 2][:, c * 128:(c + 1) * 128],
                        ident(), start=True, stop=True),
                        waits=(mmw if c4 == 0 else []),
                        inc=(cT if c4 == 3 else None))
                dst = tap(xnT, 0, 128, t * 128 + (cg * 4) * T, [[T, 4], [1, 128]])
                evacT[bank] = OP("scalar", lambda e, bank=bank, dst=dst: e.activation(
                    dst, psT[bank][:, :], AF.Copy), waits=[(cT, last)], inc=cS)
            tc = t
            bank = tc % 2
            mmw = [(dG, wvt_ld), (cS, evacV[bank]), (cS, evacT[(t * 2 + 1) % 2])]
            last = None
            for sch in range(NCH):
                for nh in range(2):
                    last = OP("tensor", lambda e, tc=tc, sch=sch, nh=nh, bank=bank: e.matmul(
                        psV[bank][:, nh * 512:(nh + 1) * 512],
                        xnT[:, sch * T + tc * 128: sch * T + (tc + 1) * 128],
                        wvt[:, sch * D + nh * 512: sch * D + (nh + 1) * 512],
                        start=(sch == 0), stop=False),
                        waits=(mmw if (sch == 0 and nh == 0) else []))
            for nh in range(2):
                last = OP("tensor", lambda e, nh=nh, bank=bank: e.matmul(
                    psV[bank][:, nh * 512:(nh + 1) * 512],
                    P["onesr"][0:1, 0:128], P["bvb"][0:1, nh * 512:(nh + 1) * 512],
                    start=False, stop=True), inc=(cT if nh == 1 else None))
            dstv = tap(vsb, 0, 128, tc * VROW, [[65, H], [1, 64]])
            evacV[bank] = OP("scalar", lambda e, dstv=dstv, bank=bank: e.activation(
                dstv, psV[bank][:, :], AF.Copy), waits=[(cT, last)], inc=cS)

        for t in range(NTC + 1):
            if t < NTC:
                stage_A(t)
            if t >= 1:
                stage_B(t - 1)
                stage_TP(t - 1)
        OP("vector", lambda e: e.reciprocal(P["fifth"][:, :], P["five"][:, :]))
        if DEBUG:
            OP("sync", lambda e: e.dma_start(dbg["dvsb"].ap(), vsb[:, :]),
               waits=[(cS, cS.v), (cV, cV.v)], inc=dM)
            OP("sync", lambda e: e.dma_start(dbg["dxnT"].ap(), xnT[:, :]), inc=dM)
        rec.flush(nc)
    wvt_cm.__exit__(None, None, None)

    p2_named = {}
    p2_cms = []
    for name, shape, dt_ in [
        ("qr", [128, 4 * T], BF16), ("kr", [128, 4 * T], BF16),
        ("qraw0", [128, T], BF16), ("qraw1", [128, T], BF16),
        ("qsq0", [128, 512], BF16), ("qsq1", [128, 512], BF16),
        ("sqs0", [128, 512], BF16), ("sqs1", [128, 512], BF16),
        ("rtmp0", [128, 512], BF16), ("rtmp1", [128, 512], BF16),
        ("rtmp20", [128, 512], BF16), ("rtmp21", [128, 512], BF16),
        ("cosb", [128, T], BF16), ("sinb", [128, T], BF16),
        ("pt0", [128, 1024], BF16), ("pt1", [128, 1024], BF16),
        ("pt2", [128, 1024], BF16),
        ("qnt", [16, 4], F32), ("qnt2", [16, 4], F32), ("knm", [16, 1], F32),
    ]:
        cm, t = alloc(name, shape, dt_)
        p2_cms.append(cm)
        p2_named[name] = t
    G = p2_named
    qraws = [G["qraw0"], G["qraw1"]]
    qsqs = [G["qsq0"], G["qsq1"]]
    sqss = [G["sqs0"], G["sqs1"]]
    rtmps = [G["rtmp0"], G["rtmp1"]]
    rtmps2 = [G["rtmp20"], G["rtmp21"]]
    pts = [G["pt0"], G["pt1"], G["pt2"]]
    ostg = P["xt1"].bitcast(BF16)
    lsts = [P["xt0"][:, 0:512], P["xt0"][:, 512:1024]]
    wqks = [P["xn0"], P["xn1"]]
    ct_p01 = cT.v

    OP("gpsimd", lambda e: e.dma_start(G["cosb"][:, :], cosf_h.ap()), inc=dG)
    tabs_ld = OP("gpsimd", lambda e: e.dma_start(G["sinb"][:, :], sinf_h.ap()), inc=dG)

    lrec_dma = [None, None]
    norm_state = {"mult_prev": [None, None], "wb_prev": [None, None]}

    def emit_norm_chunk(c, rrep_t, osb2_t, osb_t, recip_cv, extra_first=(),
                        writeback=True):
        rb = c % 2
        first_w = [(cV, recip_cv),
                   (cV, norm_state["mult_prev"][rb])] + list(extra_first)
        gsrc = bass.AP(P["Rbuf"], 8 * c * 512, [[512, 8], [1, 512]])
        gdst = bass.AP(osb2_t, 0, [[2048, 2], [1, 2048]])
        gat = OP("gpsimd", lambda e, gsrc=gsrc, gdst=gdst: e.dma_start(gdst, gsrc),
                 waits=first_w + [(dWBb[rb], norm_state["wb_prev"][rb])],
                 inc=dWQb[rb])
        bc0 = OP("gpsimd", lambda e, osb2_t=osb2_t, rrep_t=rrep_t: e.dma_start(
            tap(rrep_t, 0, 64, 0, [[1, 2048]]),
            bcast_row(osb2_t, 0, 0, 2048, 64)),
            waits=[(dWQb[rb], gat)], inc=dRRb[rb])
        bc1 = OP("gpsimd", lambda e, osb2_t=osb2_t, rrep_t=rrep_t: e.dma_start(
            tap(rrep_t, 64, 64, 0, [[1, 2048]]),
            bcast_row(osb2_t, 1, 0, 2048, 64)), inc=dWQb[rb])
        last_dma = bc0
        mw = [(dRRb[rb], bc0), (dWQb[rb], bc1)]
        mv_ = OP("vector", lambda e, c=c, rrep_t=rrep_t, osb2_t=osb2_t,
                 osb_t=osb_t: e.tensor_tensor(
            osb2_t[:, :], osb_t[:, c * T:(c + 1) * T], rrep_t[:, :], MULT),
            waits=mw, inc=cV)
        norm_state["mult_prev"][rb] = mv_
        norm_state["mult_chunk_%d" % c] = mv_
        if writeback:
            norm_state["wb_prev"][rb] = OP("sync", lambda e, c=c, osb2_t=osb2_t,
                                           osb_t=osb_t: e.dma_start(
                osb_t[:, c * T:(c + 1) * T], osb2_t[:, :]),
                waits=[(cV, mv_)], inc=dWBb[rb])
            norm_state["wb_chunk_%d" % c] = norm_state["wb_prev"][rb]
    recip_done = [None, None]
    sc_free = {0: None, 1: None, 2: None}
    pt_free = {0: None, 1: None, 2: None}
    av_free = {0: None, 1: None}
    lst_free = {0: None, 1: None}
    ost_free = {0: None, 1: None}
    crep_ready = [None, None]
    prev_wmm = [None, None]

    for g in range(2):
        with nc.psum_tensor(f"psP0_{g}", [128, 512], F32) as psP0, \
             nc.psum_tensor(f"psP1_{g}", [128, 512], F32) as psP1, \
             nc.psum_tensor(f"psR0_{g}", [128, 512], F32) as psR0, \
             nc.psum_tensor(f"psR1_{g}", [128, 512], F32) as psR1, \
             nc.psum_tensor(f"psN0_{g}", [16, 512], F32) as psN0, \
             nc.psum_tensor(f"psN1_{g}", [16, 512], F32) as psN1, \
             nc.psum_tensor(f"psN2_{g}", [16, 512], F32) as psN2, \
             nc.psum_tensor(f"psN3_{g}", [16, 512], F32) as psN3:
            psP = [psP0, psP1]
            psR = [psR0, psR1]
            psN = [psN0, psN1, psN2, psN3]

            if g == 1:
                recip_done[0] = OP("vector", lambda e: e.reciprocal(
                    P["Rbuf"][0:32, :], P["Lbuf"][0:32, :]),
                    waits=[(dLb[0], lrec_dma[0][0]), (dLb[1], lrec_dma[0][1])], inc=cV)

            sqev = {}
            normmm = {}
            ropeV = {}
            swapev = {}
            ropemul = {}
            psn_red = {"q": [None] * 4, "k": [None] * 4}
            wmm = {}
            pending_ns = None
            for ch in range(8):
                qk = "q" if ch < 4 else "k"
                cl = ch % 4
                cg = 4 * g + cl
                wh = wq_h if qk == "q" else wk_h
                dest = G["qr"] if qk == "q" else G["kr"]
                bias_sb = P["bq_sb"] if qk == "q" else P["bk_sb"]
                if ch >= 2:
                    wwaits = [(cT, wmm[ch - 2])]
                elif g == 0:
                    wwaits = [(cT, ct_p01)]
                else:
                    wwaits = [(cT, prev_wmm[ch])]
                wdma = OP("gpsimd", lambda e, wh=wh, cg=cg, ch=ch: e.dma_start(
                    wqks[ch % 2][:, :], wh.ap()[cg * 128:(cg + 1) * 128, :]),
                    waits=wwaits, inc=dWQb[ch % 2])
                for st in range(4):
                    idx = ch * 4 + st
                    bank = st % 2
                    mmw = [(dWQb[ch % 2], wdma)]
                    if idx >= 2:
                        mmw.append((cS, sqev[idx - 2]))
                    last = None
                    for s in range(NCH):
                        last = OP("tensor", lambda e, ch=ch, st=st, s=s, bank=bank: e.matmul(
                            psP[bank][:, 0:512],
                            wqks[ch % 2][:, s * 128:(s + 1) * 128],
                            xnT[:, s * T + st * 512: s * T + (st + 1) * 512],
                            start=(s == 0), stop=(s == NCH - 1)),
                            waits=(mmw if s == 0 else []),
                            inc=(cT if s == NCH - 1 else None))
                    if st == 3:
                        wmm[ch] = last
                    cwaits = [(cT, last)]
                    if ch >= 2 and st == 0:
                        cwaits.append((cV, ropeV[ch - 2]))
                    cpd = OP("scalar", lambda e, ch=ch, st=st, bank=bank,
                             bias_sb=bias_sb, cg=cg: e.activation(
                        qraws[ch % 2][:, st * 512:(st + 1) * 512],
                        psP[bank][:, 0:512], AF.Identity,
                        bias=bias_sb[:, cg:cg + 1]), waits=cwaits, inc=cS)
                    sqw = []
                    if idx >= 2:
                        sqw.append((cT, normmm[idx - 2]))
                    sqev[idx] = OP("scalar", lambda e, ch=ch, st=st, bank=bank,
                                   bias_sb=bias_sb, cg=cg: e.activation(
                        qsqs[st % 2][:, :], psP[bank][:, 0:512], AF.Square,
                        bias=bias_sb[:, cg:cg + 1]), waits=sqw, inc=cS)
                    def emit_normswap(idx=idx, ch=ch, st=st, cl=cl, cg=cg, qk=qk,
                                      dest=dest, cpd=cpd):
                        nw = [(cS, sqev[idx])]
                        if ch == 4:
                            nw.append((cV, psn_red["q"][st]))
                        normmm[idx] = OP("tensor", lambda e, st=st, cg=cg, cl=cl: e.matmul(
                            psN[st][0:16, 0:512], onesm(cg), qsqs[st % 2][:, :],
                            start=(cl == 0), stop=(cl == 3)), waits=nw, inc=cT)
                        sww = [(cS, cpd)]
                        if idx >= 2:
                            sww.append((cS, swapev[idx - 2]))
                        swmm = OP("tensor", lambda e, ch=ch, st=st: e.matmul(
                            psR[st % 2][:, 0:512], sperm(),
                            qraws[ch % 2][:, st * 512:(st + 1) * 512],
                            start=True, stop=True), waits=sww, inc=cT)
                        sew = [(cT, swmm)]
                        if idx >= 2:
                            sew.append((cV, ropemul[idx - 2]))
                        swapev[idx] = OP("scalar", lambda e, st=st: e.activation(
                            sqss[st % 2][:, :], psR[st % 2][:, 0:512], AF.Copy),
                            waits=sew, inc=cS)
                        vw = [(cS, swapev[idx])]
                        if g == 0 and ch == 0 and st == 0:
                            vw.append((dG, tabs_ld))
                        ropemul[idx] = OP("vector", lambda e, st=st: e.tensor_tensor(
                            rtmps[st % 2][:, :], sqss[st % 2][:, :],
                            G["sinb"][:, st * 512:(st + 1) * 512], MULT),
                            waits=vw, inc=cV)
                        r2v = OP("vector", lambda e, ch=ch, st=st: e.tensor_tensor(
                            rtmps2[st % 2][:, :],
                            qraws[ch % 2][:, st * 512:(st + 1) * 512],
                            G["cosb"][:, st * 512:(st + 1) * 512], MULT), inc=cV)
                        rv = OP("vector", lambda e, st=st, cl=cl, dest=dest: e.tensor_tensor(
                            dest[:, cl * T + st * 512: cl * T + (st + 1) * 512],
                            rtmps[st % 2][:, :], rtmps2[st % 2][:, :], ADD),
                            waits=[(cV, r2v)], inc=cV)
                        if st == 3:
                            ropeV[ch] = rv
                    if pending_ns is not None:
                        pending_ns()
                    pending_ns = emit_normswap
                if cl == 3:
                    if pending_ns is not None:
                        pending_ns()
                        pending_ns = None
                    dstt = G["qnt"] if qk == "q" else G["qnt2"]
                    for st in range(4):
                        psn_red[qk][st] = OP("vector", lambda e, st=st, dstt=dstt: e.tensor_reduce(
                            dstt[0:16, st:st + 1], psN[st][0:16, 0:512],
                            AX, MAXOP), waits=[(cT, normmm[ch * 4 + st])], inc=cV)

            prev_wmm = [wmm[6], wmm[7]]
            r0 = 8 * g
            knmv = OP("vector", lambda e: e.tensor_reduce(
                G["knm"][0:16, 0:1], G["qnt2"][0:16, 0:4], AX, MAXOP),
                waits=[(cV, psn_red["k"][3])], inc=cV)
            ncv = OP("vector", lambda e: e.tensor_scalar(
                G["qnt2"][0:16, 0:4], G["qnt"][0:16, 0:4],
                G["knm"][0:16, 0:1], -0.0625, ADD, MULT),
                waits=[(cV, knmv), (cV, psn_red["q"][3])], inc=cV)
            src = bass.AP(G["qnt2"], r0 * 4, [[4, 8], [1, 4]])
            d1 = OP("sync", lambda e, src=src, g=g: e.dma_start(
                bass.AP(P["negcf"], 32 * g, [[64, 1], [1, 32]]), src),
                waits=[(cV, ncv)], inc=dM)
            crep_ready[g] = OP("sync", lambda e, g=g: e.dma_start(
                tap(P["crep"], 0, 128, 32 * g, [[1, 32]]),
                bcast_row(P["negcf"], 0, 32 * g, 32, 128)),
                waits=[(dM, d1)], inc=dM)
            if DEBUG:
                OP("sync", lambda e, g=g: e.dma_start(dbg["dqr%d" % g].ap(), G["qr"][:, :]),
                   waits=[(cS, cS.v), (cV, cV.v), (dM, dM.v)], inc=dM)
                OP("sync", lambda e, g=g: e.dma_start(dbg["dkr%d" % g].ap(), G["kr"][:, :]), inc=dM)
            rec.flush(nc)

        with nc.psum_tensor(f"psS0_{g}", [128, 1024], F32) as psS0, \
             nc.psum_tensor(f"psS1_{g}", [128, 1024], F32) as psS1, \
             nc.psum_tensor(f"psS2_{g}", [128, 1024], F32) as psS2, \
             nc.psum_tensor(f"psA0_{g}", [128, 512], F32) as psA0, \
             nc.psum_tensor(f"psA1_{g}", [128, 512], F32) as psA1:
            psSc = [psS0, psS1, psS2]
            psAV = [psA0, psA1]

            def warm(n):
                pass
            nodd = 0
            if g == 1:
                norm_ex = [(dOSTb[0], dOSTb[0].v), (dOSTb[1], dOSTb[1].v), (cV, cV.v)]
            recip_half = [None]
            for hh in range(8):
                if g == 1 and 2 <= hh < 6:
                    c_ = hh - 2
                    emit_norm_chunk(c_, qraws[c_ % 2],
                                    [G["cosb"], G["sinb"]][c_ % 2], osb,
                                    recip_done[0],
                                    extra_first=(norm_ex if c_ == 0 else ()))
                if g == 1 and hh == 4:
                    recip_half[0] = OP("vector", lambda e: e.reciprocal(
                        P["Rbuf"][32:48, :], P["Lbuf"][32:48, :]),
                        waits=[(dLb[0], dLb[0].v), (dLb[1], dLb[1].v)], inc=cV)
                if g == 1 and hh in (6, 7):
                    c_ = hh - 2
                    emit_norm_chunk(c_, qraws[c_ % 2],
                                    [osb24, osb25][c_ % 2], osb,
                                    recip_half[0],
                                    extra_first=[(dOSTb[0], dOSTb[0].v),
                                                 (dOSTb[1], dOSTb[1].v)],
                                    writeback=False)
                h = 8 * g + hh
                cglob = h // 2
                cl = hh // 2
                off = (h % 2) * 64
                for qt in range(4):
                    qs = qt * 512
                    kmax = qt * 4 + 3
                    avb = (h * 4 + qt) % 2
                    exp_done = {}
                    last_av = None

                    def emit_sc_pair(pa, qt=qt, qs=qs, cl=cl, off=off, h=h, g=g):
                        b = pa % 3
                        scw = [(cS, sc_free[b])]
                        col0s = []
                        for half in range(2):
                            ki = 2 * pa + half
                            col0 = max(0, ki * 128 - qs)
                            col0s.append(col0)
                            diag = (ki * 128 >= qs)
                            OP("tensor", lambda e, cl=cl, off=off, ki=ki, qs=qs,
                               col0=col0, b=b, diag=diag, half=half: e.matmul(
                                psSc[b][:, half * 512 + col0: half * 512 + 512],
                                G["kr"][off:off + 64, cl * T + ki * 128: cl * T + (ki + 1) * 128],
                                G["qr"][off:off + 64, cl * T + qs + col0: cl * T + qs + 512],
                                start=True, stop=not diag),
                                waits=(scw if half == 0 else []),
                                inc=(None if diag else (cT if half == 1 else None)))
                            if diag:
                                OP("tensor", lambda e, b=b, col0=col0, half=half: e.matmul(
                                    psSc[b][:, half * 512 + col0: half * 512 + col0 + 128],
                                    ident(), trimask(),
                                    start=False, stop=True),
                                    inc=(cT if half == 1 else None))
                        sc = cT.v
                        expw = [(cT, sc), (cT, pt_free[b]), (dM, crep_ready[g])]
                        exp_done[pa] = OP("scalar", lambda e, b=b, c0=col0s[0], h=h,
                                          qt=qt: e.activation(
                            pts[b][:, c0:1024], psSc[b][:, c0:1024], AF.Exp,
                            bias=P["crep"][:, 4 * h + qt: 4 * h + qt + 1],
                            scale=0.125), waits=expw, inc=cS)
                        sc_free[b] = exp_done[pa]

                    def emit_av(ki, qt=qt, qs=qs, h=h, avb=avb, kmax=kmax):
                        nonlocal last_av
                        col0 = max(0, ki * 128 - qs)
                        pa = ki // 2
                        b = pa % 3
                        off2 = (ki % 2) * 512
                        avw = [(cS, exp_done[pa])]
                        if ki == 0:
                            avw.append((cV, av_free[avb]))
                        last_av = OP("tensor", lambda e, h=h, ki=ki, col0=col0, b=b,
                                     off2=off2, avb=avb, kmax=kmax: e.matmul(
                            psAV[avb][0:65, col0:512],
                            vsb[:, ki * VROW + h * 65: ki * VROW + h * 65 + 65],
                            pts[b][:, off2 + col0: off2 + 512],
                            start=(ki == 0), stop=(ki == kmax)), waits=avw, inc=cT)
                        if ki % 2 == 1:
                            pt_free[b] = last_av

                    npair = (kmax + 1) // 2
                    for pa in range(npair):
                        emit_sc_pair(pa)
                        warm(2)
                        if pa >= 2:
                            emit_av(2 * pa - 4)
                            emit_av(2 * pa - 3)
                    for ki in range(max(0, 2 * npair - 4), 2 * npair):
                        emit_av(ki)
                        warm(1)
                    lp = (h * 4 + qt) % 2
                    lcw = [(cT, last_av), (dLb[lp], lst_free[lp])]
                    lcp = OP("vector", lambda e, avb=avb, lp=lp: e.tensor_copy(
                        tap(P["xt0"], 64, 1, lp * 512, [[1, 512]]),
                        psAV[avb][64:65, 0:512]), waits=lcw, inc=cV)
                    if h % 2 == 0:
                        ocp = OP("vector", lambda e, avb=avb, cglob=cglob, qt=qt: e.tensor_copy(
                            osb[0:64, cglob * T + qt * 512: cglob * T + (qt + 1) * 512],
                            psAV[avb][0:64, 0:512]), inc=cV)
                    else:
                        ocw = [(dOSTb[0], ost_free[0])] if qt == 0 else []
                        ocp = OP("vector", lambda e, avb=avb, qt=qt: e.tensor_copy(
                            ostg[0:64, qt * 512:(qt + 1) * 512],
                            psAV[avb][0:64, 0:512]), waits=ocw, inc=cV)
                        if qt == 3:
                            ost_free[0] = OP("sync", lambda e, cglob=cglob: e.dma_start(
                                tap(osb, 64, 64, cglob * T, [[1, 2048]]),
                                ostg[0:64, 0:2048]), waits=[(cV, ocp)], inc=dOSTb[0])
                    av_free[avb] = ocp
                    lst_free[lp] = OP("sync", lambda e, lp=lp, h=h, qt=qt: e.dma_start(
                        bass.AP(P["Lbuf"], (4 * h + qt) * 512, [[512, 1], [1, 512]]),
                        tap(P["xt0"], 64, 1, lp * 512, [[1, 512]])),
                        waits=[(cV, lcp)], inc=dLb[lp])
            lrec_dma[g] = (dLb[0].v, dLb[1].v)
            rec.flush(nc)

    cs_drain = cS.v
    cv_drain = cV.v
    for cm in reversed(p2_cms):
        cm.__exit__(None, None, None)
    vsb_cm.__exit__(None, None, None)
    rrep2_cm, rrep2 = alloc("rrep2", [128, T], BF16)
    rrep3_cm, rrep3 = alloc("rrep3", [128, T], BF16)
    osb22_cm, osb22 = alloc("osb22", [128, T], BF16)
    osb23_cm, osb23 = alloc("osb23", [128, T], BF16)
    ast_cms = []
    asts = []
    for i_ in range(6):
        cm_, t_ = alloc("ast%d" % i_, [128, D], BF16)
        ast_cms.append(cm_)
        asts.append(t_)

    wpj_cm, wpj_sb = alloc("wpjsb", [128, 32 * D], BF16, side="right")
    wfcs_cm0, wfcs0 = alloc("wfcs0", [128, 1024], BF16, side="right")
    wfcs_cm1, wfcs1 = alloc("wfcs1", [128, 1024], BF16, side="right")
    wfc_pre = {}
    with nc.psum_tensor("psW0", [128, 1024], F32) as psW0, \
         nc.psum_tensor("psW1", [128, 1024], F32) as psW1:
        psW = [psW0, psW1]
        for fc0 in range(2):
            wfc_pre[fc0] = OP("gpsimd", lambda e, fc0=fc0: e.dma_start(
                [wfcs0, wfcs1][fc0][:, :], wfc_h.ap()[fc0 * 128:(fc0 + 1) * 128, :]),
                inc=dWFb[fc0])

        if DEBUG:
            OP("sync", lambda e: e.dma_start(dbg["dosbr"].ap(), osb[:, :]),
               waits=[(cS, cS.v), (dOSTb[0], dOSTb[0].v), (dOSTb[1], dOSTb[1].v),
                      (cV, cV.v)], inc=dM)
        recip_done[1] = OP("vector", lambda e: e.reciprocal(
            P["Rbuf"][32:64, :], P["Lbuf"][32:64, :]),
            waits=[(dLb[0], lrec_dma[1][0]), (dLb[1], lrec_dma[1][1])], inc=cV)
        ex = [(dOSTb[0], dOSTb[0].v), (dOSTb[1], dOSTb[1].v), (cV, cV.v)]
        for c_ in range(6, 8):
            emit_norm_chunk(c_, [rrep2, rrep3][c_ % 2], [osb22, osb23][c_ % 2],
                            osb, recip_done[1],
                            extra_first=ex, writeback=False)
        wpj_src = wpj_h.ap().rearrange("(a b) e -> b a e", a=32)
        wpj_dst = tap(wpj_sb, 0, 128, 0, [[1024, 32], [1, 1024]])
        wpj_ld = OP("gpsimd", lambda e, wpj_src=wpj_src, wpj_dst=wpj_dst: e.dma_start(
            wpj_dst, wpj_src), waits=[(cV, norm_state["mult_chunk_7"])], inc=dG)
        if DEBUG:
            OP("sync", lambda e: e.dma_start(dbg["dcosb0"].ap(), G["cosb"][:, :]),
               waits=[(cV, norm_state["mult_chunk_4"]), (dM, dM.v)], inc=dM)
        evacW = {0: None, 1: None}
        ast_dma = {}
        for tc in range(NTC):
            bank = tc % 2
            mmw = [(dG, wot_ld), (cS, evacW[bank])]
            if tc == 0:
                mmw += [(cS, cs_drain), (cV, cv_drain)]
            last = None
            for s in range(NCH):
                sw = list(mmw) if (s == 0) else []
                if s < 4:
                    sw.append((dWBb[s % 2], norm_state["wb_chunk_%d" % s]))
                else:
                    sw.append((cV, norm_state["mult_chunk_%d" % s]))
                for nh in range(2):
                    if s < 4:
                        src_t, src_off = osb, s * T
                    else:
                        src_t = {4: osb24, 5: osb25, 6: osb22, 7: osb23}[s]
                        src_off = 0
                    last = OP("tensor", lambda e, tc=tc, s=s, nh=nh, bank=bank,
                              src_t=src_t, src_off=src_off: e.matmul(
                        psW[bank][:, nh * 512:(nh + 1) * 512],
                        src_t[:, src_off + tc * 128: src_off + (tc + 1) * 128],
                        wot[:, s * D + nh * 512: s * D + (nh + 1) * 512],
                        start=(s == 0), stop=False),
                        waits=(sw if nh == 0 else []))
            for nh in range(2):
                last = OP("tensor", lambda e, nh=nh, bank=bank: e.matmul(
                    psW[bank][:, nh * 512:(nh + 1) * 512],
                    P["onesr"][0:1, 0:128], P["bob"][0:1, nh * 512:(nh + 1) * 512],
                    start=False, stop=True), inc=(cT if nh == 1 else None))
            aw = [(cT, last)]
            if tc >= 6:
                aw.append((dASb[tc % 6], ast_dma[tc - 6]))
            evacW[bank] = OP("scalar", lambda e, bank=bank, tc=tc: e.activation(
                asts[tc % 6][:, :], psW[bank][:, :], AF.Copy), waits=aw, inc=cS)
            ast_dma[tc] = OP("sync", lambda e, tc=tc: e.dma_start(
                ascr_h.ap()[tc * 128:(tc + 1) * 128, :], asts[tc % 6][:, :]),
                waits=[(cS, evacW[bank])], inc=dASb[tc % 6])
        ast_total = [c.v for c in dASb]
        evac_even_last = evacW[0]
        if DEBUG:
            OP("sync", lambda e: e.dma_start(dbg["dcosb1"].ap(), G["cosb"][:, :]),
               waits=[(cS, evacW[1]), (dM, dM.v)], inc=dM)
        rec.flush(nc)
    cs_p2d = cS.v

    for cm_ in reversed(ast_cms):
        cm_.__exit__(None, None, None)
    osb23_cm.__exit__(None, None, None)
    osb22_cm.__exit__(None, None, None)
    rrep3_cm.__exit__(None, None, None)
    rrep2_cm.__exit__(None, None, None)
    wot_cm.__exit__(None, None, None)
    osb_cm.__exit__(None, None, None)
    osb25_cm.__exit__(None, None, None)
    osb24_cm.__exit__(None, None, None)

    p3_named = {}
    p3_cms = []
    for name, shape, dt_ in [
        ("hr0", [128, 512], BF16), ("hr1", [128, 512], BF16),
        ("u_sb0", [128, 512], F32), ("u_sb1", [128, 512], F32),
        ("u2_sb0", [128, 512], F32), ("u2_sb1", [128, 512], F32),
        ("yb_sb0", [128, 512], F32), ("yb_sb1", [128, 512], F32),
        ("s_sb0", [128, 512], BF16), ("s_sb1", [128, 512], BF16),
        ("acc_sb0", [128, 512], BF16), ("acc_sb1", [128, 512], BF16),
        ("acc2_sb0", [128, 512], BF16), ("acc2_sb1", [128, 512], BF16),
        ("hsb", [128, 32 * SUPT], BF16),
    ]:
        cm, t = alloc(name, shape, dt_)
        p3_cms.append(cm)
        p3_named[name] = t
    Q = p3_named
    wfcs = [wfcs0, wfcs1]
    hrs = [Q["hr0"], Q["hr1"]]
    xnTf = xnT.bitcast(F32)
    US2 = [xnTf[:, s * (T // 2): s * (T // 2) + 512] for s in range(4)]
    BS2 = [xnT[:, s * T + h * 512: s * T + (h + 1) * 512]
           for s in (4, 5, 6) for h in range(2)]
    xt2s = [xts[0][:, 0:512], xts[0][:, 512:1024]]
    outsts = [xts[1][:, 0:512], xts[1][:, 512:1024]]
    at2s = [xns[0][:, 0:512], xns[0][:, 512:1024]]
    hsb = Q["hsb"]

    with nc.psum_tensor("psF0", [128, 512], F32) as psF0, \
         nc.psum_tensor("psF1", [128, 512], F32) as psF1, \
         nc.psum_tensor("psJ0", [128, 512], F32) as psJ0, \
         nc.psum_tensor("psJ1", [128, 512], F32) as psJ1, \
         nc.psum_tensor("psJ2", [128, 512], F32) as psJ2, \
         nc.psum_tensor("psJ3", [128, 512], F32) as psJ3, \
         nc.psum_tensor("psJ4", [128, 512], F32) as psJ4, \
         nc.psum_tensor("psJ5", [128, 512], F32) as psJ5:
        psF = [psF0, psF1]
        psJ = [psJ0, psJ1, psJ2, psJ3, psJ4, psJ5]
        psF_free = [None, None]
        ngroup = 0
        wfc_mm = {}
        pending_hsq = None
        hr_free = {}
        psj_free = [None] * 6
        xt2_free = {0: None, 1: None}
        at2_free = {0: None, 1: None}
        outst_free = {0: None, 1: None}
        u_free = [None, None, None, None]
        ntile = 0
        n_out = 0
        tile_loads = {}

        def issue_tile_load(su, eh, tg, tl):
            nonlocal tile_loads
            tb = (su * 16 + eh * 8 + tg * 4 + tl) % 2
            rows = su * SUPT + tg * 512 + tl * 128
            cols = eh * 512
            w1 = [(cV, xt2_free[tb])]
            if xt2_free[tb] is None:
                w1 += [(dLb[0], lrec_dma[1][0]), (dLb[1], lrec_dma[1][1])]
            l1 = OP("sync", lambda e, tb=tb, rows=rows, cols=cols: e.dma_start(
                xt2s[tb], x_h.ap()[rows:rows + 128, cols:cols + 512]),
                waits=w1, inc=dXb[tb])
            l2 = OP("sync", lambda e, tb=tb, rows=rows, cols=cols: e.dma_start(
                at2s[tb], ascr_h.ap()[rows:rows + 128, cols:cols + 512]),
                waits=[(dASb[i_], ast_total[i_]) for i_ in range(6)],
                inc=dATb[tb])
            tile_loads[(su, eh, tg, tl)] = (tb, l1, l2)

        for su in range(NSUP):
            for fc in range(32):
                fb = fc % 2
                if su == 0 and fc < 2:
                    wdma = wfc_pre[fc]
                else:
                    wwaits = [(cT, wfc_mm[(su * 32 + fc - 2)])]
                    wdma = OP("gpsimd", lambda e, fc=fc, fb=fb: e.dma_start(
                        wfcs[fb][:, :], wfc_h.ap()[fc * 128:(fc + 1) * 128, :]),
                        waits=wwaits, inc=dWFb[fb])
                for th in range(2):
                    pb = (fc * 2 + th) % 2
                    mmw = [(dWFb[fb], wdma), (cS, psF_free[pb])]
                    if su == 0 and fc == 0:
                        mmw.append((cS, evac_even_last))
                    last = None
                    for s in range(NCH):
                        last = OP("tensor", lambda e, su=su, fc=fc, th=th, s=s, pb=pb, fb=fb: e.matmul(
                            psF[pb][:, 0:512],
                            wfcs[fb][:, s * 128:(s + 1) * 128],
                            xnT[:, s * T + su * SUPT + th * 512: s * T + su * SUPT + (th + 1) * 512],
                            start=(s == 0), stop=(s == NCH - 1)),
                            waits=(mmw if s == 0 else []),
                            inc=(cT if s == NCH - 1 else None))
                    wfc_mm[su * 32 + fc] = last
                    hb = (fc * 2 + th) % 2
                    rw = [(cT, last), (cV, hr_free.get(hb))]
                    hrel = OP("scalar", lambda e, pb=pb, hb=hb: e.activation(
                        hrs[hb][:, :], psF[pb][:, 0:512], AF.Relu),
                        waits=rw, inc=cS)
                    psF_free[pb] = hrel
                    if pending_hsq is not None:
                        pending_hsq()

                    def mk_hsq(fc=fc, th=th, hb=hb, hrel=hrel, su=su):
                        def go():
                            hw = [(cS, hrel)]
                            if su == 0 and 16 <= fc <= 31:
                                hw += [(dASb[i_], ast_total[i_]) for i_ in range(6)]
                            hsq_v = OP("vector", lambda e, fc=fc, th=th, hb=hb: e.tensor_tensor(
                                hsb[:, fc * SUPT + th * 512: fc * SUPT + (th + 1) * 512],
                                hrs[hb][:, :], hrs[hb][:, :], MULT), waits=hw, inc=cV)
                            hr_free[hb] = hsq_v
                        return go
                    pending_hsq = mk_hsq()
            if pending_hsq is not None:
                pending_hsq()
                pending_hsq = None
            h_ready = cV.v
            for eh in range(2):
                for tg in range(2):
                    for tl in range(2):
                        issue_tile_load(su, eh, tg, tl)
                    slots = [(ngroup * 4 + tl) % 6 for tl in range(4)]
                    ngroup += 1
                    pj_last = [None] * 4
                    for fc in range(32):
                        for tl in range(4):
                            sl = slots[tl]
                            mmw = [(cV, h_ready), (dG, wpj_ld)]
                            if fc == 0:
                                mmw.append((cS, psj_free[sl]))
                            pj_last[tl] = OP("tensor", lambda e, su=su, eh=eh, tg=tg,
                                             fc=fc, tl=tl, sl=sl: e.matmul(
                                psJ[sl][:, 0:512],
                                hsb[:, fc * SUPT + (tg * 4 + tl) * 128: fc * SUPT + (tg * 4 + tl) * 128 + 128],
                                wpj_sb[:, fc * D + eh * 512: fc * D + (eh + 1) * 512],
                                start=(fc == 0), stop=(fc == 31)),
                                waits=(mmw if (fc == 0 or tl == 0) else []),
                                inc=(cT if fc == 31 else None))
                    US = [Q["u_sb0"], Q["u_sb1"], US2[0], US2[1]]
                    U2S = [Q["u2_sb0"], Q["u2_sb1"]]
                    YBS = [Q["yb_sb0"], Q["yb_sb1"], US2[2], US2[3]]
                    SS = [Q["s_sb0"], Q["s_sb1"], BS2[0], BS2[1]]
                    ACCS = [[Q["acc_sb0"], Q["acc2_sb0"]],
                            [Q["acc_sb1"], Q["acc2_sb1"]],
                            [BS2[2], BS2[4]],
                            [BS2[3], BS2[5]]]
                    for pair in range(2):
                        tls = (2 * pair, 2 * pair + 1)
                        yev = {}
                        for tl in tls:
                            ntile += 1
                            sl = slots[tl]
                            uw = [(cT, pj_last[tl]), (cV, u_free[tl])]
                            OP("scalar", lambda e, sl=sl, tl=tl: e.activation(
                                US[tl][:, :], psJ[sl][:, 0:512], AF.Identity, bias=-1.0),
                                waits=uw)
                            psj_free[sl] = OP("scalar", lambda e, sl=sl, tl=tl: e.activation(
                                YBS[tl][:, :], psJ[sl][:, 0:512], AF.Identity, bias=-0.1),
                                inc=cS)
                            yev[tl] = psj_free[sl]
                        for tl in tls:
                            OP("vector", lambda e, tl=tl: e.tensor_single_scalar(
                                SS[tl][:, :], US[tl][:, :], 0.0, ISGT),
                                waits=[(cS, yev[tl])])
                        for tl in tls:
                            OP("vector", lambda e, tl=tl: e.tensor_copy(
                                ACCS[tl][0][:, :], SS[tl][:, :]))
                        ai = 0
                        for _step in range(4):
                            for tl in tls:
                                OP("vector", lambda e, tl=tl: e.scalar_tensor_tensor(
                                    U2S[tl % 2][:, :], US[tl][:, :], 0.9,
                                    YBS[tl][:, :], MULT, ADD))
                            for tl in tls:
                                OP("vector", lambda e, tl=tl: e.tensor_tensor(
                                    US[tl][:, :], U2S[tl % 2][:, :], SS[tl][:, :], SUB))
                            for tl in tls:
                                OP("vector", lambda e, tl=tl: e.tensor_single_scalar(
                                    SS[tl][:, :], US[tl][:, :], 0.0, ISGT))
                            for tl in tls:
                                OP("vector", lambda e, tl=tl, ai=ai: e.tensor_tensor(
                                    ACCS[tl][1 - ai][:, :], ACCS[tl][ai][:, :],
                                    SS[tl][:, :], ADD))
                            ai = 1 - ai
                        einfo = {}
                        for tl in tls:
                            tb, l1, l2 = tile_loads[(su, eh, tg, tl)]
                            ob = n_out % 2
                            n_out += 1
                            einfo[tl] = (tb, ob)
                            ew = [(dXb[tb], l1), (dATb[tb], l2),
                                  (dOUb[ob], outst_free[ob])]
                            OP("vector", lambda e, tb=tb, tl=tl, ai=ai: e.scalar_tensor_tensor(
                                U2S[tl % 2][:, :], ACCS[tl][ai][:, :], P["fifth"][:, 0:1],
                                at2s[tb], MULT, ADD), waits=ew)
                        for tl in tls:
                            tb, ob = einfo[tl]
                            vout = OP("vector", lambda e, tb=tb, ob=ob, tl=tl: e.tensor_tensor(
                                outsts[ob], U2S[tl % 2][:, :], xt2s[tb], ADD),
                                inc=cV)
                            xt2_free[tb] = vout
                            at2_free[tb] = vout
                            u_free[tl] = vout
                            rows = su * SUPT + tg * 512 + tl * 128
                            cols = eh * 512
                            outst_free[ob] = OP("sync", lambda e, ob=ob, rows=rows,
                                                cols=cols: e.dma_start(
                                out_h.ap()[rows:rows + 128, cols:cols + 512],
                                outsts[ob]), waits=[(cV, vout)], inc=dOUb[ob])
                        if pair == 0:
                            issue_tile_load(su, eh, tg, 2)
                            issue_tile_load(su, eh, tg, 3)
        if DEBUG:
            OP("sync", lambda e: e.dma_start(dbg["dhsb"].ap(), hsb[:, :]),
               waits=[(cV, cV.v)], inc=dM)
        OP("sync", lambda e: None,
           waits=[(dOUb[0], dOUb[0].v), (dOUb[1], dOUb[1].v)])
        rec.flush(nc)

    return nc


_CACHED = {}


def _host_prep(L0_attn_WQ_w, L0_attn_WQ_b, L0_attn_WK_w, L0_attn_WK_b,
               L0_attn_WV_w, L0_attn_WV_b, L0_attn_Wo_w, L0_attn_Wo_b,
               L0_ffn_W_fc, L0_ffn_W_proj):
    f32 = np.float32
    WQ = np.asarray(L0_attn_WQ_w, f32)
    WK = np.asarray(L0_attn_WK_w, f32)
    WV = np.asarray(L0_attn_WV_w, f32)
    WO = np.asarray(L0_attn_Wo_w, f32)
    Wfc = np.asarray(L0_ffn_W_fc, f32)
    Wpj = np.asarray(L0_ffn_W_proj, f32)

    def qk_tile(W):
        t = W.reshape(8, 128, 8, 128).transpose(0, 3, 2, 1)
        return np.ascontiguousarray(t).reshape(D, D)

    def de_tile(W):
        t = W.reshape(D, 8, 128).transpose(2, 1, 0)
        return np.ascontiguousarray(t).reshape(128, 8 * D)

    wq = qk_tile(WQ)
    wk = qk_tile(WK)
    wv = de_tile(WV)
    wo = de_tile(WO)
    wfc = np.ascontiguousarray(
        Wfc.reshape(32, 128, 8, 128).transpose(0, 3, 2, 1)).reshape(FF, D)
    wpj = np.ascontiguousarray(
        Wpj.reshape(D, 32, 128).transpose(1, 2, 0)).reshape(FF, D)

    bq = np.ascontiguousarray(np.asarray(L0_attn_WQ_b, f32).reshape(8, 128).T)
    bk = np.ascontiguousarray(np.asarray(L0_attn_WK_b, f32).reshape(8, 128).T)
    bv = np.asarray(L0_attn_WV_b, f32).reshape(1, D)
    bo = np.asarray(L0_attn_Wo_b, f32).reshape(1, D)

    half = DK // 2
    i = np.arange(half, dtype=np.float64)
    freq = 1.0 / 10000.0 ** (2.0 * i / DK)
    tpos = np.arange(T, dtype=np.float64)
    ang = tpos[None, :] * freq[:, None]
    cos32 = np.cos(ang)
    sin32 = np.sin(ang)
    cosf = np.empty((128, T), np.float64)
    sinf = np.empty((128, T), np.float64)
    for blk in range(4):
        cosf[blk * 32:(blk + 1) * 32] = cos32
        sgn = -1.0 if blk % 2 == 0 else 1.0
        sinf[blk * 32:(blk + 1) * 32] = sgn * sin32
    cosf = cosf.astype(f32)
    sinf = sinf.astype(f32)

    cb = np.zeros((128, 4 * 128), bfloat16)
    cb[:, 0:128] = np.eye(128, dtype=f32).astype(bfloat16)
    sp = np.zeros((128, 128), f32)
    for p in range(128):
        base = (p // 64) * 64
        d = p % 64
        sp[base + (d + 32) % 64, p] = 1.0
    cb[:, 128:256] = sp.astype(bfloat16)
    tri = np.where(np.arange(128)[None, :] < np.arange(128)[:, None],
                   f32(-30000.0), f32(0.0))
    cb[:, 256:384] = tri.astype(bfloat16)
    om = np.zeros((128, 128), f32)
    for c in range(8):
        om[0:64, c * 16 + 2 * c] = 1.0
        om[64:128, c * 16 + 2 * c + 1] = 1.0
    cb[:, 384:512] = om.astype(bfloat16)
    onesrow = np.ones((1, 128), bfloat16)

    return dict(wq=wq, wk=wk, wv=wv, wo=wo, wfc=wfc, wpj=wpj,
                bqv=bq, bkv=bk, bvv=bv, bov=bo, cosf=cosf, sinf=sinf,
                cb=cb, onesrow=onesrow,
                cvals_in=np.broadcast_to(
                    np.array([1e-5, 2e-5, -1.0, -0.1, 0.125, 0.0, 1.0, 5.0], f32),
                    (128, 8)).copy())


def kernel(**inputs):
    x = np.ascontiguousarray(np.asarray(inputs["L0_attn_x"], np.float32))
    shared = _host_prep(
        inputs["L0_attn_WQ_w"], inputs["L0_attn_WQ_b"],
        inputs["L0_attn_WK_w"], inputs["L0_attn_WK_b"],
        inputs["L0_attn_WV_w"], inputs["L0_attn_WV_b"],
        inputs["L0_attn_Wo_w"], inputs["L0_attn_Wo_b"],
        inputs["L0_ffn_W_fc"], inputs["L0_ffn_W_proj"])
    nc = build_nc()
    in_maps = []
    for i in range(8):
        m = dict(shared)
        m["x"] = np.ascontiguousarray(x[i])
        in_maps.append(m)
    res = run_bass_kernel_spmd(nc, in_maps, core_ids=list(range(8)))
    return np.stack([np.asarray(res.results[i]["out"]) for i in range(8)], axis=0)


if __name__ == "__main__":
    nc = build_nc()
    print("built:", nc.n_instructions(), "instructions")


# version 30
# speedup vs baseline: 1.0047x; 1.0047x over previous
import numpy as np
import ml_dtypes
import concourse.bass as bass
import concourse.mybir as mybir
from concourse.bass_utils import run_bass_kernel_spmd
from concourse.alu_op_type import AluOpType

F32 = mybir.dt.float32
BF16 = mybir.dt.bfloat16
AF = mybir.ActivationFunctionType
AX = mybir.AxisListType.X
bfloat16 = ml_dtypes.bfloat16
MULT = AluOpType.mult
ADD = AluOpType.add
SUB = AluOpType.subtract
MAXOP = AluOpType.max
MINOP = AluOpType.min
ISGT = AluOpType.is_gt

D = 1024
T = 2048
H = 16
DK = 64
FF = 4096
NCH = 8
NTC = 16
VROW = H * 65
NSUP = 2
SUPT = 1024
DEBUG = False


class Ctr:
    def __init__(self, nc, name, step=1):
        self._cm = nc.semaphore(name)
        self.h = self._cm.__enter__()
        self.step = step
        self.v = 0


class Rec:
    def __init__(self):
        self.reset()

    def reset(self):
        self.ops = {"sync": [], "tensor": [], "scalar": [], "vector": [], "gpsimd": []}
        self.waited = {k: {} for k in self.ops}

    def op(self, eng, fn, waits=(), inc=None):
        w = []
        for (c, v) in waits:
            if v is None or v <= 0:
                continue
            key = id(c)
            if self.waited[eng].get(key, -1) >= v:
                continue
            self.waited[eng][key] = v
            w.append((c.h, v))
        if inc is not None:
            inc.v += inc.step

        def replay(engine, fn=fn, w=tuple(w), inc=inc):
            for (h, v) in w:
                engine.wait_ge(h, v)
            inst = fn(engine)
            if inc is not None:
                inst.then_inc(inc.h, inc.step)
        self.ops[eng].append(replay)
        return inc.v if inc is not None else None

    def flush(self, nc):
        if not any(self.ops.values()):
            return
        with nc.Block() as block:
            for name in ("sync", "tensor", "scalar", "vector", "gpsimd"):
                lst = self.ops[name]
                if lst:
                    def body(engine, lst=tuple(lst)):
                        for f in lst:
                            f(engine)
                    getattr(block, name)(body)
        self.reset()


def tap(t, part0, nparts, off, dims):
    W = int(t.shape[1])
    return bass.AP(t, part0 * W + off, [[W, nparts]] + [list(d) for d in dims])


def bcast_row(t, row, off, w, nrep):
    W = int(t.shape[1])
    return bass.AP(t, row * W + off, [[W, 1], [0, nrep], [1, w]])


def build_nc():
    nc = bass.Bass("TRN2", target_bir_lowering=False)

    x_h = nc.dram_tensor("x", [T, D], F32, kind="ExternalInput")
    wq_h = nc.dram_tensor("wq", [D, D], F32, kind="ExternalInput")
    wk_h = nc.dram_tensor("wk", [D, D], F32, kind="ExternalInput")
    wv_h = nc.dram_tensor("wv", [128, 8 * D], F32, kind="ExternalInput")
    wo_h = nc.dram_tensor("wo", [128, 8 * D], F32, kind="ExternalInput")
    wfc_h = nc.dram_tensor("wfc", [FF, D], F32, kind="ExternalInput")
    wpj_h = nc.dram_tensor("wpj", [FF, D], F32, kind="ExternalInput")
    bq_h = nc.dram_tensor("bqv", [128, 8], F32, kind="ExternalInput")
    bk_h = nc.dram_tensor("bkv", [128, 8], F32, kind="ExternalInput")
    bv_h = nc.dram_tensor("bvv", [1, D], F32, kind="ExternalInput")
    bo_h = nc.dram_tensor("bov", [1, D], F32, kind="ExternalInput")
    cosf_h = nc.dram_tensor("cosf", [128, T], F32, kind="ExternalInput")
    sinf_h = nc.dram_tensor("sinf", [128, T], F32, kind="ExternalInput")
    cb_h = nc.dram_tensor("cb", [128, 4 * 128], BF16, kind="ExternalInput")
    onesrow_h = nc.dram_tensor("onesrow", [1, 128], BF16, kind="ExternalInput")
    cvals_h = nc.dram_tensor("cvals_in", [128, 8], F32, kind="ExternalInput")
    out_h = nc.dram_tensor("out", [T, D], F32, kind="ExternalOutput")
    ascr_h = nc.dram_tensor("ascr", [T, D], BF16,
                            kind="ExternalOutput" if DEBUG else "Internal")
    if DEBUG:
        dbg = {
            "dxnT": nc.dram_tensor("dxnT", [128, NCH * T], BF16, kind="ExternalOutput"),
            "dvsb": nc.dram_tensor("dvsb", [128, NTC * VROW], BF16, kind="ExternalOutput"),
            "dqr0": nc.dram_tensor("dqr0", [128, 4 * T], BF16, kind="ExternalOutput"),
            "dkr0": nc.dram_tensor("dkr0", [128, 4 * T], BF16, kind="ExternalOutput"),
            "dqr1": nc.dram_tensor("dqr1", [128, 4 * T], BF16, kind="ExternalOutput"),
            "dkr1": nc.dram_tensor("dkr1", [128, 4 * T], BF16, kind="ExternalOutput"),
            "dcrep": nc.dram_tensor("dcrep", [128, 64], F32, kind="ExternalOutput"),
            "dLbuf": nc.dram_tensor("dLbuf", [64, 512], F32, kind="ExternalOutput"),
            "dRbuf": nc.dram_tensor("dRbuf", [64, 512], F32, kind="ExternalOutput"),
            "dosb": nc.dram_tensor("dosb", [128, NCH * T], BF16, kind="ExternalOutput"),
            "dhsb": nc.dram_tensor("dhsb", [128, 32 * SUPT], BF16, kind="ExternalOutput"),
            "dqn": nc.dram_tensor("dqn", [16, T], BF16, kind="ExternalOutput"),
            "dkn": nc.dram_tensor("dkn", [16, T], BF16, kind="ExternalOutput"),
            "dmvh": nc.dram_tensor("dmvh", [128, 64], F32, kind="ExternalOutput"),
            "dosbr": nc.dram_tensor("dosbr", [128, NCH * T], BF16, kind="ExternalOutput"),
            "dcosb0": nc.dram_tensor("dcosb0", [128, T], BF16, kind="ExternalOutput"),
            "dcosb1": nc.dram_tensor("dcosb1", [128, T], BF16, kind="ExternalOutput"),
            "drr0": nc.dram_tensor("drr0", [128, T], BF16, kind="ExternalOutput"),
            "drr1": nc.dram_tensor("drr1", [128, T], BF16, kind="ExternalOutput"),
            "dpool": nc.dram_tensor("dpool", [128, 32], F32, kind="ExternalOutput"),
            "dmv": nc.dram_tensor("dmv", [128, 32], F32, kind="ExternalOutput"),
            "dsqv": nc.dram_tensor("dsqv", [128, 16], F32, kind="ExternalOutput"),
            "dmsum": nc.dram_tensor("dmsum", [128, 16], F32, kind="ExternalOutput"),
            "drstd": nc.dram_tensor("drstd", [128, 16], F32, kind="ExternalOutput"),
            "dnegms": nc.dram_tensor("dnegms", [128, 16], F32, kind="ExternalOutput"),
        }

    dM = Ctr(nc, "dM", 16)
    dG = Ctr(nc, "dG", 16)
    dXb = [Ctr(nc, "dX0", 16), Ctr(nc, "dX1", 16)]
    dATb = [Ctr(nc, "dAT0", 16), Ctr(nc, "dAT1", 16)]
    dASb = [Ctr(nc, "dAS%d" % i_, 16) for i_ in range(6)]
    dLb = [Ctr(nc, "dL0", 16), Ctr(nc, "dL1", 16)]
    dOSTb = [Ctr(nc, "dOST0", 16), Ctr(nc, "dOST1", 16)]
    dWQb = [Ctr(nc, "dWQ0", 16), Ctr(nc, "dWQ1", 16)]
    dWFb = [Ctr(nc, "dWF0", 16), Ctr(nc, "dWF1", 16)]
    dRRb = [Ctr(nc, "dRR0", 16), Ctr(nc, "dRR1", 16)]
    dWBb = [Ctr(nc, "dWB0", 16), Ctr(nc, "dWB1", 16)]
    dOUb = [Ctr(nc, "dOU0", 16), Ctr(nc, "dOU1", 16)]
    cT = Ctr(nc, "cT")
    cS = Ctr(nc, "cS")
    cV = Ctr(nc, "cV")
    cG = Ctr(nc, "cG")

    rec = Rec()
    OP = rec.op

    def alloc(name, shape, dtype, side=None):
        if side is None:
            cm = nc.sbuf_tensor(name, shape, dtype)
        else:
            cm = nc.sbuf_tensor(name, shape, dtype, side=side)
        return cm, cm.__enter__()

    P = {}
    cms = []
    for name, shape, dt_ in [
        ("xnT", [128, NCH * T], BF16),
        ("xt0", [128, D], F32), ("xt1", [128, D], F32),
        ("xn0", [128, D], BF16), ("xn1", [128, D], BF16),
        ("bnout", [128, 12], F32), ("mv", [128, 32], F32),
        ("mvh", [128, 64], F32), ("msum", [128, 16], F32),
        ("pool", [128, 32], F32), ("rstdf", [128, 16], F32),
        ("sqv", [128, 16], F32), ("rstd", [128, 16], F32), ("negms", [128, 16], F32),
        ("bq_sb", [128, 8], F32), ("bk_sb", [128, 8], F32),
        ("bvb", [1, D], BF16), ("bob", [1, D], BF16),
        ("cbc", [128, 4 * 128], BF16), ("onesr", [1, 128], BF16),
        ("Lbuf", [64, 512], F32), ("Rbuf", [64, 512], F32),
        ("crep", [128, 64], F32), ("negcf", [1, 64], F32),
        ("five", [128, 1], F32), ("fifth", [128, 1], F32),
        ("cvals", [128, 8], F32),
    ]:
        cm, t = alloc(name, shape, dt_)
        cms.append(cm)
        P[name] = t
    xnT = P["xnT"]
    xts = [P["xt0"], P["xt1"]]
    xns = [P["xn0"], P["xn1"]]
    cbc = P["cbc"]

    def ident():
        return cbc[:, 0:128]

    def sperm():
        return cbc[:, 128:256]

    def trimask():
        return cbc[:, 256:384]

    def onesm(c):
        return cbc[:, 384 + c * 16:384 + (c + 1) * 16]

    CVALS = [1e-5, 2e-5, -1.0, -0.1, 0.125, 0.0, 1.0, 5.0]
    for i, val in enumerate(CVALS):
        nc.const_aps.register(F32, val, P["cvals"][:, i:i + 1])

    osb24_cm, osb24 = alloc("osb24", [128, T], BF16)
    osb25_cm, osb25 = alloc("osb25", [128, T], BF16)
    osb_cm, osb = alloc("osb", [128, NCH * T], BF16)
    wot_cm, wot = alloc("wot", [128, 8 * D], BF16)
    vsb_cm, vsb = alloc("vsb", [128, NTC * VROW], BF16)
    wvt_cm, wvt = alloc("wvt", [128, 8 * D], BF16, side="right")

    OP("sync", lambda e: e.dma_start(P["bq_sb"][:, :], bq_h.ap()), inc=dM)
    OP("sync", lambda e: e.dma_start(P["bk_sb"][:, :], bk_h.ap()), inc=dM)
    OP("sync", lambda e: e.dma_start(cbc[:, :], cb_h.ap()), inc=dM)
    cl0 = OP("sync", lambda e: e.dma_start(P["onesr"][:, :], onesrow_h.ap()), inc=dM)
    cvld = OP("sync", lambda e: e.dma_start(P["cvals"][:, :], cvals_h.ap()), inc=dM)
    OP("gpsimd", lambda e: e.dma_start(P["bvb"][:, :], bv_h.ap()), inc=dG)
    OP("gpsimd", lambda e: e.dma_start(P["bob"][:, :], bo_h.ap()), inc=dG)
    wvt_ld = OP("gpsimd", lambda e: e.dma_start(wvt[:, :], wv_h.ap()), inc=dG)
    wot_ld = OP("gpsimd", lambda e: e.dma_start(wot[:, :], wo_h.ap()), inc=dG)

    with nc.psum_tensor("psT0", [128, 512], F32) as psT0, \
         nc.psum_tensor("psT1", [128, 512], F32) as psT1, \
         nc.psum_tensor("psV0", [128, 1024], F32) as psV0, \
         nc.psum_tensor("psV1", [128, 1024], F32) as psV1:
        psT = [psT0, psT1]
        psV = [psV0, psV1]
        ones_ap = bass.AP(vsb, 64, [[NTC * VROW, 128], [VROW, NTC], [65, H], [1, 1]])
        OP("vector", lambda e: e.memset(P["five"][:, :], 5.0))
        OP("vector", lambda e: e.memset(ones_ap, 1.0))
        evacT = {0: None, 1: None}
        evacV = {0: None, 1: None}
        xn_done = [None] * NTC
        sqd_l = [None] * NTC

        def stage_A(t):
            w = [(cS, xn_done[t - 2])] if t >= 2 else []
            ld = OP("sync", lambda e, t=t: e.dma_start(
                xts[t % 2][:, :], x_h.ap()[t * 128:(t + 1) * 128, :]), waits=w,
                inc=dXb[t % 2])
            OP("vector", lambda e, t=t: e.bn_stats(
                P["bnout"][:, 0:6], xts[t % 2][:, 0:512]), waits=[(dXb[t % 2], ld)])
            OP("vector", lambda e, t=t: e.bn_stats(
                P["bnout"][:, 6:12], xts[t % 2][:, 512:1024]))
            OP("vector", lambda e, t=t: e.bn_aggr(
                P["mvh"][:, 4 * t:4 * t + 2], P["bnout"][:, 0:6]))
            ag2 = OP("vector", lambda e, t=t: e.bn_aggr(
                P["mvh"][:, 4 * t + 2:4 * t + 4], P["bnout"][:, 6:12]), inc=cV)
            mc = 4 * t
            md = OP("vector", lambda e, mc=mc, t=t: e.tensor_tensor(
                P["mv"][:, t:t + 1], P["mvh"][:, mc:mc + 1],
                P["mvh"][:, mc + 2:mc + 3], SUB), waits=[(cV, ag2)], inc=cV)
            OP("vector", lambda e, mc=mc, t=t: e.tensor_tensor(
                P["msum"][:, t:t + 1], P["mvh"][:, mc:mc + 1],
                P["mvh"][:, mc + 2:mc + 3], ADD))
            msq = OP("vector", lambda e, t=t: e.tensor_tensor(
                P["pool"][:, t:t + 1], P["mv"][:, t:t + 1], P["mv"][:, t:t + 1],
                MULT), waits=[(cV, md)], inc=cV)
            cd = OP("vector", lambda e, mc=mc, t=t: e.scalar_tensor_tensor(
                P["pool"][:, 16 + t:17 + t], P["pool"][:, t:t + 1], 0.5,
                P["mvh"][:, mc + 1:mc + 2], MULT, ADD), waits=[(cV, msq)], inc=cV)
            vd = OP("vector", lambda e, mc=mc, t=t: e.tensor_tensor(
                P["mv"][:, 16 + t:17 + t], P["pool"][:, 16 + t:17 + t],
                P["mvh"][:, mc + 3:mc + 4], ADD), waits=[(cV, cd)], inc=cV)
            sqd_l[t] = OP("scalar", lambda e, t=t: e.activation(
                P["sqv"][:, t:t + 1], P["mv"][:, 16 + t:17 + t], AF.Sqrt,
                bias=2e-5), waits=[(cV, vd), (dM, cvld)], inc=cS)

        def stage_B(t):
            rr = OP("vector", lambda e, t=t: e.reciprocal(
                P["rstd"][:, t:t + 1], P["sqv"][:, t:t + 1]),
                waits=[(cS, sqd_l[t])], inc=cV)
            OP("vector", lambda e, t=t: e.scalar_tensor_tensor(
                P["negms"][:, t:t + 1], P["msum"][:, t:t + 1],
                -0.7071067811865476,
                P["rstd"][:, t:t + 1], MULT, MULT), waits=[(cV, rr)])
            sf = OP("vector", lambda e, t=t: e.tensor_single_scalar(
                P["rstdf"][:, t:t + 1], P["rstd"][:, t:t + 1],
                1.4142135623730951, MULT), inc=cV)
            xn_done[t] = OP("scalar", lambda e, t=t: e.activation(
                xns[t % 2][:, :], xts[t % 2][:, :], AF.Identity,
                bias=P["negms"][:, t:t + 1], scale=P["rstdf"][:, t:t + 1]),
                waits=[(cV, sf)], inc=cS)

        def stage_TP(t):
            for cg in range(2):
                bank = (t * 2 + cg) % 2
                mmw = [(cS, xn_done[t]), (cS, evacT[bank])]
                last = None
                for c4 in range(4):
                    c = cg * 4 + c4
                    last = OP("tensor", lambda e, t=t, c=c, c4=c4, bank=bank: e.matmul(
                        psT[bank][:, c4 * 128:(c4 + 1) * 128],
                        xns[t % 2][:, c * 128:(c + 1) * 128],
                        ident(), start=True, stop=True),
                        waits=(mmw if c4 == 0 else []),
                        inc=(cT if c4 == 3 else None))
                dst = tap(xnT, 0, 128, t * 128 + (cg * 4) * T, [[T, 4], [1, 128]])
                evacT[bank] = OP("scalar", lambda e, bank=bank, dst=dst: e.activation(
                    dst, psT[bank][:, :], AF.Copy), waits=[(cT, last)], inc=cS)
            tc = t
            bank = tc % 2
            mmw = [(dG, wvt_ld), (cS, evacV[bank]), (cS, evacT[(t * 2 + 1) % 2])]
            last = None
            for sch in range(NCH):
                for nh in range(2):
                    last = OP("tensor", lambda e, tc=tc, sch=sch, nh=nh, bank=bank: e.matmul(
                        psV[bank][:, nh * 512:(nh + 1) * 512],
                        xnT[:, sch * T + tc * 128: sch * T + (tc + 1) * 128],
                        wvt[:, sch * D + nh * 512: sch * D + (nh + 1) * 512],
                        start=(sch == 0), stop=False),
                        waits=(mmw if (sch == 0 and nh == 0) else []))
            for nh in range(2):
                last = OP("tensor", lambda e, nh=nh, bank=bank: e.matmul(
                    psV[bank][:, nh * 512:(nh + 1) * 512],
                    P["onesr"][0:1, 0:128], P["bvb"][0:1, nh * 512:(nh + 1) * 512],
                    start=False, stop=True), inc=(cT if nh == 1 else None))
            dstv = tap(vsb, 0, 128, tc * VROW, [[65, H], [1, 64]])
            evacV[bank] = OP("scalar", lambda e, dstv=dstv, bank=bank: e.activation(
                dstv, psV[bank][:, :], AF.Copy), waits=[(cT, last)], inc=cS)

        for t in range(NTC + 1):
            if t < NTC:
                stage_A(t)
            if t >= 1:
                stage_B(t - 1)
                stage_TP(t - 1)
        OP("vector", lambda e: e.reciprocal(P["fifth"][:, :], P["five"][:, :]))
        if DEBUG:
            OP("sync", lambda e: e.dma_start(dbg["dvsb"].ap(), vsb[:, :]),
               waits=[(cS, cS.v), (cV, cV.v)], inc=dM)
            OP("sync", lambda e: e.dma_start(dbg["dxnT"].ap(), xnT[:, :]), inc=dM)
        rec.flush(nc)
    wvt_cm.__exit__(None, None, None)

    p2_named = {}
    p2_cms = []
    for name, shape, dt_ in [
        ("qr", [128, 4 * T], BF16), ("kr", [128, 4 * T], BF16),
        ("qraw0", [128, T], BF16), ("qraw1", [128, T], BF16),
        ("qsq0", [128, 512], BF16), ("qsq1", [128, 512], BF16),
        ("sqs0", [128, 512], BF16), ("sqs1", [128, 512], BF16),
        ("rtmp0", [128, 512], BF16), ("rtmp1", [128, 512], BF16),
        ("rtmp20", [128, 512], BF16), ("rtmp21", [128, 512], BF16),
        ("cosb", [128, T], BF16), ("sinb", [128, T], BF16),
        ("pt0", [128, 1024], BF16), ("pt1", [128, 1024], BF16),
        ("pt2", [128, 1024], BF16),
        ("qnt", [16, 4], F32), ("qnt2", [16, 4], F32), ("knm", [16, 1], F32),
    ]:
        cm, t = alloc(name, shape, dt_)
        p2_cms.append(cm)
        p2_named[name] = t
    G = p2_named
    qraws = [G["qraw0"], G["qraw1"]]
    qsqs = [G["qsq0"], G["qsq1"]]
    sqss = [G["sqs0"], G["sqs1"]]
    rtmps = [G["rtmp0"], G["rtmp1"]]
    rtmps2 = [G["rtmp20"], G["rtmp21"]]
    pts = [G["pt0"], G["pt1"], G["pt2"]]
    ostg = P["xt1"].bitcast(BF16)
    lsts = [P["xt0"][:, 0:512], P["xt0"][:, 512:1024]]
    wqks = [P["xn0"], P["xn1"]]
    ct_p01 = cT.v

    OP("gpsimd", lambda e: e.dma_start(G["cosb"][:, :], cosf_h.ap()), inc=dG)
    tabs_ld = OP("gpsimd", lambda e: e.dma_start(G["sinb"][:, :], sinf_h.ap()), inc=dG)

    lrec_dma = [None, None]
    norm_state = {"mult_prev": [None, None], "wb_prev": [None, None]}

    def emit_norm_chunk(c, rrep_t, osb2_t, osb_t, recip_cv, extra_first=(),
                        writeback=True):
        rb = c % 2
        first_w = [(cV, recip_cv),
                   (cV, norm_state["mult_prev"][rb])] + list(extra_first)
        gsrc = bass.AP(P["Rbuf"], 8 * c * 512, [[512, 8], [1, 512]])
        gdst = bass.AP(osb2_t, 0, [[2048, 2], [1, 2048]])
        gat = OP("gpsimd", lambda e, gsrc=gsrc, gdst=gdst: e.dma_start(gdst, gsrc),
                 waits=first_w + [(dWBb[rb], norm_state["wb_prev"][rb])],
                 inc=dWQb[rb])
        bc0 = OP("gpsimd", lambda e, osb2_t=osb2_t, rrep_t=rrep_t: e.dma_start(
            tap(rrep_t, 0, 64, 0, [[1, 2048]]),
            bcast_row(osb2_t, 0, 0, 2048, 64)),
            waits=[(dWQb[rb], gat)], inc=dRRb[rb])
        bc1 = OP("gpsimd", lambda e, osb2_t=osb2_t, rrep_t=rrep_t: e.dma_start(
            tap(rrep_t, 64, 64, 0, [[1, 2048]]),
            bcast_row(osb2_t, 1, 0, 2048, 64)), inc=dWQb[rb])
        last_dma = bc0
        mw = [(dRRb[rb], bc0), (dWQb[rb], bc1)]
        mv_ = OP("vector", lambda e, c=c, rrep_t=rrep_t, osb2_t=osb2_t,
                 osb_t=osb_t: e.tensor_tensor(
            osb2_t[:, :], osb_t[:, c * T:(c + 1) * T], rrep_t[:, :], MULT),
            waits=mw, inc=cV)
        norm_state["mult_prev"][rb] = mv_
        norm_state["mult_chunk_%d" % c] = mv_
        if writeback:
            norm_state["wb_prev"][rb] = OP("sync", lambda e, c=c, osb2_t=osb2_t,
                                           osb_t=osb_t: e.dma_start(
                osb_t[:, c * T:(c + 1) * T], osb2_t[:, :]),
                waits=[(cV, mv_)], inc=dWBb[rb])
            norm_state["wb_chunk_%d" % c] = norm_state["wb_prev"][rb]
    recip_done = [None, None]
    sc_free = {0: None, 1: None, 2: None}
    pt_free = {0: None, 1: None, 2: None}
    av_free = {0: None, 1: None}
    lst_free = {0: None, 1: None}
    ost_free = {0: None, 1: None}
    crep_ready = [None, None]
    prev_wmm = [None, None]

    for g in range(2):
        with nc.psum_tensor(f"psP0_{g}", [128, 512], F32) as psP0, \
             nc.psum_tensor(f"psP1_{g}", [128, 512], F32) as psP1, \
             nc.psum_tensor(f"psR0_{g}", [128, 512], F32) as psR0, \
             nc.psum_tensor(f"psR1_{g}", [128, 512], F32) as psR1, \
             nc.psum_tensor(f"psN0_{g}", [16, 512], F32) as psN0, \
             nc.psum_tensor(f"psN1_{g}", [16, 512], F32) as psN1, \
             nc.psum_tensor(f"psN2_{g}", [16, 512], F32) as psN2, \
             nc.psum_tensor(f"psN3_{g}", [16, 512], F32) as psN3:
            psP = [psP0, psP1]
            psR = [psR0, psR1]
            psN = [psN0, psN1, psN2, psN3]

            if g == 1:
                recip_done[0] = OP("vector", lambda e: e.reciprocal(
                    P["Rbuf"][0:32, :], P["Lbuf"][0:32, :]),
                    waits=[(dLb[0], lrec_dma[0][0]), (dLb[1], lrec_dma[0][1])], inc=cV)

            sqev = {}
            normmm = {}
            ropeV = {}
            swapev = {}
            ropemul = {}
            psn_red = {"q": [None] * 4, "k": [None] * 4}
            wmm = {}
            pending_ns = None
            for ch in range(8):
                qk = "q" if ch < 4 else "k"
                cl = ch % 4
                cg = 4 * g + cl
                wh = wq_h if qk == "q" else wk_h
                dest = G["qr"] if qk == "q" else G["kr"]
                bias_sb = P["bq_sb"] if qk == "q" else P["bk_sb"]
                if ch >= 2:
                    wwaits = [(cT, wmm[ch - 2])]
                elif g == 0:
                    wwaits = [(cT, ct_p01)]
                else:
                    wwaits = [(cT, prev_wmm[ch])]
                wdma = OP("gpsimd", lambda e, wh=wh, cg=cg, ch=ch: e.dma_start(
                    wqks[ch % 2][:, :], wh.ap()[cg * 128:(cg + 1) * 128, :]),
                    waits=wwaits, inc=dWQb[ch % 2])
                for st in range(4):
                    idx = ch * 4 + st
                    bank = st % 2
                    mmw = [(dWQb[ch % 2], wdma)]
                    if idx >= 2:
                        mmw.append((cS, sqev[idx - 2]))
                    last = None
                    for s in range(NCH):
                        last = OP("tensor", lambda e, ch=ch, st=st, s=s, bank=bank: e.matmul(
                            psP[bank][:, 0:512],
                            wqks[ch % 2][:, s * 128:(s + 1) * 128],
                            xnT[:, s * T + st * 512: s * T + (st + 1) * 512],
                            start=(s == 0), stop=(s == NCH - 1)),
                            waits=(mmw if s == 0 else []),
                            inc=(cT if s == NCH - 1 else None))
                    if st == 3:
                        wmm[ch] = last
                    cwaits = [(cT, last)]
                    if ch >= 2 and st == 0:
                        cwaits.append((cV, ropeV[ch - 2]))
                    cpd = OP("scalar", lambda e, ch=ch, st=st, bank=bank,
                             bias_sb=bias_sb, cg=cg: e.activation(
                        qraws[ch % 2][:, st * 512:(st + 1) * 512],
                        psP[bank][:, 0:512], AF.Identity,
                        bias=bias_sb[:, cg:cg + 1]), waits=cwaits, inc=cS)
                    sqw = []
                    if idx >= 2:
                        sqw.append((cT, normmm[idx - 2]))
                    sqev[idx] = OP("scalar", lambda e, ch=ch, st=st, bank=bank,
                                   bias_sb=bias_sb, cg=cg: e.activation(
                        qsqs[st % 2][:, :], psP[bank][:, 0:512], AF.Square,
                        bias=bias_sb[:, cg:cg + 1]), waits=sqw, inc=cS)
                    def emit_normswap(idx=idx, ch=ch, st=st, cl=cl, cg=cg, qk=qk,
                                      dest=dest, cpd=cpd):
                        nw = [(cS, sqev[idx])]
                        if ch == 4:
                            nw.append((cV, psn_red["q"][st]))
                        normmm[idx] = OP("tensor", lambda e, st=st, cg=cg, cl=cl: e.matmul(
                            psN[st][0:16, 0:512], onesm(cg), qsqs[st % 2][:, :],
                            start=(cl == 0), stop=(cl == 3)), waits=nw, inc=cT)
                        sww = [(cS, cpd)]
                        if idx >= 2:
                            sww.append((cS, swapev[idx - 2]))
                        swmm = OP("tensor", lambda e, ch=ch, st=st: e.matmul(
                            psR[st % 2][:, 0:512], sperm(),
                            qraws[ch % 2][:, st * 512:(st + 1) * 512],
                            start=True, stop=True), waits=sww, inc=cT)
                        sew = [(cT, swmm)]
                        if idx >= 2:
                            sew.append((cV, ropemul[idx - 2]))
                        swapev[idx] = OP("scalar", lambda e, st=st: e.activation(
                            sqss[st % 2][:, :], psR[st % 2][:, 0:512], AF.Copy),
                            waits=sew, inc=cS)
                        vw = [(cS, swapev[idx])]
                        if g == 0 and ch == 0 and st == 0:
                            vw.append((dG, tabs_ld))
                        ropemul[idx] = OP("vector", lambda e, st=st: e.tensor_tensor(
                            rtmps[st % 2][:, :], sqss[st % 2][:, :],
                            G["sinb"][:, st * 512:(st + 1) * 512], MULT),
                            waits=vw, inc=cV)
                        r2v = OP("vector", lambda e, ch=ch, st=st: e.tensor_tensor(
                            rtmps2[st % 2][:, :],
                            qraws[ch % 2][:, st * 512:(st + 1) * 512],
                            G["cosb"][:, st * 512:(st + 1) * 512], MULT), inc=cV)
                        rv = OP("vector", lambda e, st=st, cl=cl, dest=dest: e.tensor_tensor(
                            dest[:, cl * T + st * 512: cl * T + (st + 1) * 512],
                            rtmps[st % 2][:, :], rtmps2[st % 2][:, :], ADD),
                            waits=[(cV, r2v)], inc=cV)
                        if st == 3:
                            ropeV[ch] = rv
                    if pending_ns is not None:
                        pending_ns()
                    pending_ns = emit_normswap
                if cl == 3:
                    if pending_ns is not None:
                        pending_ns()
                        pending_ns = None
                    dstt = G["qnt"] if qk == "q" else G["qnt2"]
                    for st in range(4):
                        psn_red[qk][st] = OP("vector", lambda e, st=st, dstt=dstt: e.tensor_reduce(
                            dstt[0:16, st:st + 1], psN[st][0:16, 0:512],
                            AX, MAXOP), waits=[(cT, normmm[ch * 4 + st])], inc=cV)

            prev_wmm = [wmm[6], wmm[7]]
            r0 = 8 * g
            knmv = OP("vector", lambda e: e.tensor_reduce(
                G["knm"][0:16, 0:1], G["qnt2"][0:16, 0:4], AX, MAXOP),
                waits=[(cV, psn_red["k"][3])], inc=cV)
            ncv = OP("vector", lambda e: e.tensor_scalar(
                G["qnt2"][0:16, 0:4], G["qnt"][0:16, 0:4],
                G["knm"][0:16, 0:1], -0.0625, ADD, MULT),
                waits=[(cV, knmv), (cV, psn_red["q"][3])], inc=cV)
            src = bass.AP(G["qnt2"], r0 * 4, [[4, 8], [1, 4]])
            d1 = OP("sync", lambda e, src=src, g=g: e.dma_start(
                bass.AP(P["negcf"], 32 * g, [[64, 1], [1, 32]]), src),
                waits=[(cV, ncv)], inc=dM)
            crep_ready[g] = OP("sync", lambda e, g=g: e.dma_start(
                tap(P["crep"], 0, 128, 32 * g, [[1, 32]]),
                bcast_row(P["negcf"], 0, 32 * g, 32, 128)),
                waits=[(dM, d1)], inc=dM)
            if DEBUG:
                OP("sync", lambda e, g=g: e.dma_start(dbg["dqr%d" % g].ap(), G["qr"][:, :]),
                   waits=[(cS, cS.v), (cV, cV.v), (dM, dM.v)], inc=dM)
                OP("sync", lambda e, g=g: e.dma_start(dbg["dkr%d" % g].ap(), G["kr"][:, :]), inc=dM)
            rec.flush(nc)

        with nc.psum_tensor(f"psS0_{g}", [128, 1024], F32) as psS0, \
             nc.psum_tensor(f"psS1_{g}", [128, 1024], F32) as psS1, \
             nc.psum_tensor(f"psS2_{g}", [128, 1024], F32) as psS2, \
             nc.psum_tensor(f"psA0_{g}", [128, 512], F32) as psA0, \
             nc.psum_tensor(f"psA1_{g}", [128, 512], F32) as psA1:
            psSc = [psS0, psS1, psS2]
            psAV = [psA0, psA1]

            def warm(n):
                pass
            nodd = 0
            if g == 1:
                norm_ex = [(dOSTb[0], dOSTb[0].v), (dOSTb[1], dOSTb[1].v), (cV, cV.v)]
            recip_half = [None]
            for hh in range(8):
                if g == 1 and 2 <= hh < 6:
                    c_ = hh - 2
                    emit_norm_chunk(c_, qraws[c_ % 2],
                                    [G["cosb"], G["sinb"]][c_ % 2], osb,
                                    recip_done[0],
                                    extra_first=(norm_ex if c_ == 0 else ()))
                if g == 1 and hh == 4:
                    recip_half[0] = OP("vector", lambda e: e.reciprocal(
                        P["Rbuf"][32:48, :], P["Lbuf"][32:48, :]),
                        waits=[(dLb[0], dLb[0].v), (dLb[1], dLb[1].v)], inc=cV)
                if g == 1 and hh in (6, 7):
                    c_ = hh - 2
                    emit_norm_chunk(c_, qraws[c_ % 2],
                                    [osb24, osb25][c_ % 2], osb,
                                    recip_half[0],
                                    extra_first=[(dOSTb[0], dOSTb[0].v),
                                                 (dOSTb[1], dOSTb[1].v)],
                                    writeback=False)
                h = 8 * g + hh
                cglob = h // 2
                cl = hh // 2
                off = (h % 2) * 64
                for qt in range(4):
                    qs = qt * 512
                    kmax = qt * 4 + 3
                    avb = (h * 4 + qt) % 2
                    exp_done = {}
                    last_av = None

                    def emit_sc_pair(pa, qt=qt, qs=qs, cl=cl, off=off, h=h, g=g):
                        b = pa % 3
                        scw = [(cS, sc_free[b])]
                        col0s = []
                        for half in range(2):
                            ki = 2 * pa + half
                            col0 = max(0, ki * 128 - qs)
                            col0s.append(col0)
                            diag = (ki * 128 >= qs)
                            OP("tensor", lambda e, cl=cl, off=off, ki=ki, qs=qs,
                               col0=col0, b=b, diag=diag, half=half: e.matmul(
                                psSc[b][:, half * 512 + col0: half * 512 + 512],
                                G["kr"][off:off + 64, cl * T + ki * 128: cl * T + (ki + 1) * 128],
                                G["qr"][off:off + 64, cl * T + qs + col0: cl * T + qs + 512],
                                start=True, stop=not diag),
                                waits=(scw if half == 0 else []),
                                inc=(None if diag else (cT if half == 1 else None)))
                            if diag:
                                OP("tensor", lambda e, b=b, col0=col0, half=half: e.matmul(
                                    psSc[b][:, half * 512 + col0: half * 512 + col0 + 128],
                                    ident(), trimask(),
                                    start=False, stop=True),
                                    inc=(cT if half == 1 else None))
                        sc = cT.v
                        expw = [(cT, sc), (cT, pt_free[b]), (dM, crep_ready[g])]
                        exp_done[pa] = OP("scalar", lambda e, b=b, c0=col0s[0], h=h,
                                          qt=qt: e.activation(
                            pts[b][:, c0:1024], psSc[b][:, c0:1024], AF.Exp,
                            bias=P["crep"][:, 4 * h + qt: 4 * h + qt + 1],
                            scale=0.125), waits=expw, inc=cS)
                        sc_free[b] = exp_done[pa]

                    def emit_av(ki, qt=qt, qs=qs, h=h, avb=avb, kmax=kmax):
                        nonlocal last_av
                        col0 = max(0, ki * 128 - qs)
                        pa = ki // 2
                        b = pa % 3
                        off2 = (ki % 2) * 512
                        avw = [(cS, exp_done[pa])]
                        if ki == 0:
                            avw.append((cV, av_free[avb]))
                        last_av = OP("tensor", lambda e, h=h, ki=ki, col0=col0, b=b,
                                     off2=off2, avb=avb, kmax=kmax: e.matmul(
                            psAV[avb][0:65, col0:512],
                            vsb[:, ki * VROW + h * 65: ki * VROW + h * 65 + 65],
                            pts[b][:, off2 + col0: off2 + 512],
                            start=(ki == 0), stop=(ki == kmax)), waits=avw, inc=cT)
                        if ki % 2 == 1:
                            pt_free[b] = last_av

                    npair = (kmax + 1) // 2
                    for pa in range(npair):
                        emit_sc_pair(pa)
                        warm(2)
                        if pa >= 2:
                            emit_av(2 * pa - 4)
                            emit_av(2 * pa - 3)
                    for ki in range(max(0, 2 * npair - 4), 2 * npair):
                        emit_av(ki)
                        warm(1)
                    lp = (h * 4 + qt) % 2
                    lcw = [(cT, last_av), (dLb[lp], lst_free[lp])]
                    lcp = OP("vector", lambda e, avb=avb, lp=lp: e.tensor_copy(
                        tap(P["xt0"], 64, 1, lp * 512, [[1, 512]]),
                        psAV[avb][64:65, 0:512]), waits=lcw, inc=cV)
                    if h % 2 == 0:
                        ocp = OP("vector", lambda e, avb=avb, cglob=cglob, qt=qt: e.tensor_copy(
                            osb[0:64, cglob * T + qt * 512: cglob * T + (qt + 1) * 512],
                            psAV[avb][0:64, 0:512]), inc=cV)
                    else:
                        ocw = [(dOSTb[0], ost_free[0])] if qt == 0 else []
                        ocp = OP("vector", lambda e, avb=avb, qt=qt: e.tensor_copy(
                            ostg[0:64, qt * 512:(qt + 1) * 512],
                            psAV[avb][0:64, 0:512]), waits=ocw, inc=cV)
                        if qt == 3:
                            ost_free[0] = OP("sync", lambda e, cglob=cglob: e.dma_start(
                                tap(osb, 64, 64, cglob * T, [[1, 2048]]),
                                ostg[0:64, 0:2048]), waits=[(cV, ocp)], inc=dOSTb[0])
                    av_free[avb] = ocp
                    lst_free[lp] = OP("sync", lambda e, lp=lp, h=h, qt=qt: e.dma_start(
                        bass.AP(P["Lbuf"], (4 * h + qt) * 512, [[512, 1], [1, 512]]),
                        tap(P["xt0"], 64, 1, lp * 512, [[1, 512]])),
                        waits=[(cV, lcp)], inc=dLb[lp])
            lrec_dma[g] = (dLb[0].v, dLb[1].v)
            rec.flush(nc)

    cs_drain = cS.v
    cv_drain = cV.v
    for cm in reversed(p2_cms):
        cm.__exit__(None, None, None)
    vsb_cm.__exit__(None, None, None)
    rrep2_cm, rrep2 = alloc("rrep2", [128, T], BF16)
    rrep3_cm, rrep3 = alloc("rrep3", [128, T], BF16)
    osb22_cm, osb22 = alloc("osb22", [128, T], BF16)
    osb23_cm, osb23 = alloc("osb23", [128, T], BF16)
    ast_cms = []
    asts = []
    for i_ in range(6):
        cm_, t_ = alloc("ast%d" % i_, [128, D], BF16)
        ast_cms.append(cm_)
        asts.append(t_)

    wpj_cm, wpj_sb = alloc("wpjsb", [128, 32 * D], BF16, side="right")
    wfcs_cm0, wfcs0 = alloc("wfcs0", [128, 1024], BF16, side="right")
    wfcs_cm1, wfcs1 = alloc("wfcs1", [128, 1024], BF16, side="right")
    wfc_pre = {}
    with nc.psum_tensor("psW0", [128, 1024], F32) as psW0, \
         nc.psum_tensor("psW1", [128, 1024], F32) as psW1:
        psW = [psW0, psW1]
        for fc0 in range(2):
            wfc_pre[fc0] = OP("gpsimd", lambda e, fc0=fc0: e.dma_start(
                [wfcs0, wfcs1][fc0][:, :], wfc_h.ap()[fc0 * 128:(fc0 + 1) * 128, :]),
                inc=dWFb[fc0])

        if DEBUG:
            OP("sync", lambda e: e.dma_start(dbg["dosbr"].ap(), osb[:, :]),
               waits=[(cS, cS.v), (dOSTb[0], dOSTb[0].v), (dOSTb[1], dOSTb[1].v),
                      (cV, cV.v)], inc=dM)
        recip_done[1] = OP("vector", lambda e: e.reciprocal(
            P["Rbuf"][32:64, :], P["Lbuf"][32:64, :]),
            waits=[(dLb[0], lrec_dma[1][0]), (dLb[1], lrec_dma[1][1])], inc=cV)
        ex = [(dOSTb[0], dOSTb[0].v), (dOSTb[1], dOSTb[1].v), (cV, cV.v)]
        for c_ in range(6, 8):
            emit_norm_chunk(c_, [rrep2, rrep3][c_ % 2], [osb22, osb23][c_ % 2],
                            osb, recip_done[1],
                            extra_first=ex, writeback=False)
        wpj_src = wpj_h.ap().rearrange("(a b) e -> b a e", a=32)
        wpj_dst = tap(wpj_sb, 0, 128, 0, [[1024, 32], [1, 1024]])
        wpj_ld = OP("gpsimd", lambda e, wpj_src=wpj_src, wpj_dst=wpj_dst: e.dma_start(
            wpj_dst, wpj_src), waits=[(cV, norm_state["mult_chunk_7"])], inc=dG)
        if DEBUG:
            OP("sync", lambda e: e.dma_start(dbg["dcosb0"].ap(), G["cosb"][:, :]),
               waits=[(cV, norm_state["mult_chunk_4"]), (dM, dM.v)], inc=dM)
        evacW = {0: None, 1: None}
        ast_dma = {}
        for tc in range(NTC):
            bank = tc % 2
            mmw = [(dG, wot_ld), (cS, evacW[bank])]
            if tc == 0:
                mmw += [(cS, cs_drain), (cV, cv_drain)]
            last = None
            for s in range(NCH):
                sw = list(mmw) if (s == 0) else []
                if s < 4:
                    sw.append((dWBb[s % 2], norm_state["wb_chunk_%d" % s]))
                else:
                    sw.append((cV, norm_state["mult_chunk_%d" % s]))
                for nh in range(2):
                    if s < 4:
                        src_t, src_off = osb, s * T
                    else:
                        src_t = {4: osb24, 5: osb25, 6: osb22, 7: osb23}[s]
                        src_off = 0
                    last = OP("tensor", lambda e, tc=tc, s=s, nh=nh, bank=bank,
                              src_t=src_t, src_off=src_off: e.matmul(
                        psW[bank][:, nh * 512:(nh + 1) * 512],
                        src_t[:, src_off + tc * 128: src_off + (tc + 1) * 128],
                        wot[:, s * D + nh * 512: s * D + (nh + 1) * 512],
                        start=(s == 0), stop=False),
                        waits=(sw if nh == 0 else []))
            for nh in range(2):
                last = OP("tensor", lambda e, nh=nh, bank=bank: e.matmul(
                    psW[bank][:, nh * 512:(nh + 1) * 512],
                    P["onesr"][0:1, 0:128], P["bob"][0:1, nh * 512:(nh + 1) * 512],
                    start=False, stop=True), inc=(cT if nh == 1 else None))
            aw = [(cT, last)]
            if tc >= 6:
                aw.append((dASb[tc % 6], ast_dma[tc - 6]))
            evacW[bank] = OP("scalar", lambda e, bank=bank, tc=tc: e.activation(
                asts[tc % 6][:, :], psW[bank][:, :], AF.Copy), waits=aw, inc=cS)
            ast_dma[tc] = OP("sync", lambda e, tc=tc: e.dma_start(
                ascr_h.ap()[tc * 128:(tc + 1) * 128, :], asts[tc % 6][:, :]),
                waits=[(cS, evacW[bank])], inc=dASb[tc % 6])
        ast_total = [c.v for c in dASb]
        evac_even_last = evacW[0]
        if DEBUG:
            OP("sync", lambda e: e.dma_start(dbg["dcosb1"].ap(), G["cosb"][:, :]),
               waits=[(cS, evacW[1]), (dM, dM.v)], inc=dM)
        rec.flush(nc)
    cs_p2d = cS.v

    for cm_ in reversed(ast_cms):
        cm_.__exit__(None, None, None)
    osb23_cm.__exit__(None, None, None)
    osb22_cm.__exit__(None, None, None)
    rrep3_cm.__exit__(None, None, None)
    rrep2_cm.__exit__(None, None, None)
    wot_cm.__exit__(None, None, None)
    osb_cm.__exit__(None, None, None)
    osb25_cm.__exit__(None, None, None)
    osb24_cm.__exit__(None, None, None)

    p3_named = {}
    p3_cms = []
    for name, shape, dt_ in [
        ("hr0", [128, 512], BF16), ("hr1", [128, 512], BF16),
        ("u_sb0", [128, 512], F32), ("u_sb1", [128, 512], F32),
        ("u2_sb0", [128, 512], F32), ("u2_sb1", [128, 512], F32),
        ("yb_sb0", [128, 512], F32), ("yb_sb1", [128, 512], F32),
        ("s_sb0", [128, 512], BF16), ("s_sb1", [128, 512], BF16),
        ("acc_sb0", [128, 512], BF16), ("acc_sb1", [128, 512], BF16),
        ("acc2_sb0", [128, 512], BF16), ("acc2_sb1", [128, 512], BF16),
        ("hsb", [128, 32 * SUPT], BF16),
    ]:
        cm, t = alloc(name, shape, dt_)
        p3_cms.append(cm)
        p3_named[name] = t
    Q = p3_named
    wfcs = [wfcs0, wfcs1]
    hrs = [Q["hr0"], Q["hr1"]]
    xnTf = xnT.bitcast(F32)
    US2 = [xnTf[:, s * (T // 2): s * (T // 2) + 512] for s in range(4)]
    BS2 = [xnT[:, s * T + h * 512: s * T + (h + 1) * 512]
           for s in (4, 5, 6) for h in range(2)]
    xt2s = [xts[0][:, 0:512], xts[0][:, 512:1024]]
    outsts = [xts[1][:, 0:512], xts[1][:, 512:1024]]
    at2s = [xns[0][:, 0:512], xns[0][:, 512:1024]]
    hsb = Q["hsb"]

    with nc.psum_tensor("psF0", [128, 512], F32) as psF0, \
         nc.psum_tensor("psF1", [128, 512], F32) as psF1, \
         nc.psum_tensor("psJ0", [128, 512], F32) as psJ0, \
         nc.psum_tensor("psJ1", [128, 512], F32) as psJ1, \
         nc.psum_tensor("psJ2", [128, 512], F32) as psJ2, \
         nc.psum_tensor("psJ3", [128, 512], F32) as psJ3, \
         nc.psum_tensor("psJ4", [128, 512], F32) as psJ4, \
         nc.psum_tensor("psJ5", [128, 512], F32) as psJ5:
        psF = [psF0, psF1]
        psJ = [psJ0, psJ1, psJ2, psJ3, psJ4, psJ5]
        psF_free = [None, None]
        ngroup = 0
        wfc_mm = {}
        pending_hsq = None
        hr_free = {}
        psj_free = [None] * 6
        xt2_free = {0: None, 1: None}
        at2_free = {0: None, 1: None}
        outst_free = {0: None, 1: None}
        u_free = [None, None, None, None]
        g_acc_last = [None, None, None, None]
        ntile = 0
        n_out = 0
        tile_loads = {}

        def issue_tile_load(su, eh, tg, tl):
            nonlocal tile_loads
            tb = (su * 16 + eh * 8 + tg * 4 + tl) % 2
            rows = su * SUPT + tg * 512 + tl * 128
            cols = eh * 512
            w1 = [(cV, xt2_free[tb])]
            if xt2_free[tb] is None:
                w1 += [(dLb[0], lrec_dma[1][0]), (dLb[1], lrec_dma[1][1])]
            l1 = OP("sync", lambda e, tb=tb, rows=rows, cols=cols: e.dma_start(
                xt2s[tb], x_h.ap()[rows:rows + 128, cols:cols + 512]),
                waits=w1, inc=dXb[tb])
            l2 = OP("sync", lambda e, tb=tb, rows=rows, cols=cols: e.dma_start(
                at2s[tb], ascr_h.ap()[rows:rows + 128, cols:cols + 512]),
                waits=[(dASb[i_], ast_total[i_]) for i_ in range(6)],
                inc=dATb[tb])
            tile_loads[(su, eh, tg, tl)] = (tb, l1, l2)

        for su in range(NSUP):
            for fc in range(32):
                fb = fc % 2
                if su == 0 and fc < 2:
                    wdma = wfc_pre[fc]
                else:
                    wwaits = [(cT, wfc_mm[(su * 32 + fc - 2)])]
                    wdma = OP("gpsimd", lambda e, fc=fc, fb=fb: e.dma_start(
                        wfcs[fb][:, :], wfc_h.ap()[fc * 128:(fc + 1) * 128, :]),
                        waits=wwaits, inc=dWFb[fb])
                for th in range(2):
                    pb = (fc * 2 + th) % 2
                    mmw = [(dWFb[fb], wdma), (cS, psF_free[pb])]
                    if su == 0 and fc == 0:
                        mmw.append((cS, evac_even_last))
                    last = None
                    for s in range(NCH):
                        last = OP("tensor", lambda e, su=su, fc=fc, th=th, s=s, pb=pb, fb=fb: e.matmul(
                            psF[pb][:, 0:512],
                            wfcs[fb][:, s * 128:(s + 1) * 128],
                            xnT[:, s * T + su * SUPT + th * 512: s * T + su * SUPT + (th + 1) * 512],
                            start=(s == 0), stop=(s == NCH - 1)),
                            waits=(mmw if s == 0 else []),
                            inc=(cT if s == NCH - 1 else None))
                    wfc_mm[su * 32 + fc] = last
                    hb = (fc * 2 + th) % 2
                    rw = [(cT, last), (cV, hr_free.get(hb))]
                    hrel = OP("scalar", lambda e, pb=pb, hb=hb: e.activation(
                        hrs[hb][:, :], psF[pb][:, 0:512], AF.Relu),
                        waits=rw, inc=cS)
                    psF_free[pb] = hrel
                    if pending_hsq is not None:
                        pending_hsq()

                    def mk_hsq(fc=fc, th=th, hb=hb, hrel=hrel, su=su):
                        def go():
                            hw = [(cS, hrel)]
                            if su == 0 and 16 <= fc <= 31:
                                hw += [(dASb[i_], ast_total[i_]) for i_ in range(6)]
                            hsq_v = OP("vector", lambda e, fc=fc, th=th, hb=hb: e.tensor_tensor(
                                hsb[:, fc * SUPT + th * 512: fc * SUPT + (th + 1) * 512],
                                hrs[hb][:, :], hrs[hb][:, :], MULT), waits=hw, inc=cV)
                            hr_free[hb] = hsq_v
                        return go
                    pending_hsq = mk_hsq()
            if pending_hsq is not None:
                pending_hsq()
                pending_hsq = None
            h_ready = cV.v
            for eh in range(2):
                for tg in range(2):
                    for tl in range(2):
                        issue_tile_load(su, eh, tg, tl)
                    slots = [(ngroup * 4 + tl) % 6 for tl in range(4)]
                    ngroup += 1
                    pj_last = [None] * 4
                    for fc in range(32):
                        for tl in range(4):
                            sl = slots[tl]
                            mmw = [(cV, h_ready), (dG, wpj_ld)]
                            if fc == 0:
                                mmw.append((cS, psj_free[sl]))
                            pj_last[tl] = OP("tensor", lambda e, su=su, eh=eh, tg=tg,
                                             fc=fc, tl=tl, sl=sl: e.matmul(
                                psJ[sl][:, 0:512],
                                hsb[:, fc * SUPT + (tg * 4 + tl) * 128: fc * SUPT + (tg * 4 + tl) * 128 + 128],
                                wpj_sb[:, fc * D + eh * 512: fc * D + (eh + 1) * 512],
                                start=(fc == 0), stop=(fc == 31)),
                                waits=(mmw if (fc == 0 or tl == 0) else []),
                                inc=(cT if fc == 31 else None))
                    US = [Q["u_sb0"], Q["u_sb1"], US2[0], US2[1]]
                    U2S = [Q["u2_sb0"], Q["u2_sb1"]]
                    YBS = [Q["yb_sb0"], Q["yb_sb1"], US2[2], US2[3]]
                    SS = [Q["s_sb0"], Q["s_sb1"], BS2[0], BS2[1]]
                    ACCS = [[Q["acc_sb0"], Q["acc2_sb0"]],
                            [Q["acc_sb1"], Q["acc2_sb1"]],
                            [BS2[2], BS2[4]],
                            [BS2[3], BS2[5]]]
                    for pair in range(2):
                        tls = (2 * pair, 2 * pair + 1)
                        yev = {}
                        for tl in tls:
                            ntile += 1
                            sl = slots[tl]
                            uw = [(cT, pj_last[tl]), (cV, u_free[tl])]
                            OP("scalar", lambda e, sl=sl, tl=tl: e.activation(
                                US[tl][:, :], psJ[sl][:, 0:512], AF.Identity, bias=-1.0),
                                waits=uw)
                            psj_free[sl] = OP("scalar", lambda e, sl=sl, tl=tl: e.activation(
                                YBS[tl][:, :], psJ[sl][:, 0:512], AF.Identity, bias=-0.1),
                                inc=cS)
                            yev[tl] = psj_free[sl]
                        on_g = (pair == 0)
                        isgt_v = {}
                        for tl in tls:
                            sw = [(cS, yev[tl])]
                            if on_g:
                                sw.append((cG, g_acc_last[tl]))
                            isgt_v[tl] = OP("vector", lambda e, tl=tl: e.tensor_single_scalar(
                                SS[tl][:, :], US[tl][:, :], 0.0, ISGT),
                                waits=sw, inc=(cV if on_g else None))
                        for tl in tls:
                            if on_g:
                                g_acc_last[tl] = OP("gpsimd", lambda e, tl=tl: e.tensor_copy(
                                    ACCS[tl][0][:, :], SS[tl][:, :]),
                                    waits=[(cV, isgt_v[tl]), (cV, u_free[tl])], inc=cG)
                            else:
                                OP("vector", lambda e, tl=tl: e.tensor_copy(
                                    ACCS[tl][0][:, :], SS[tl][:, :]))
                        ai = 0
                        for _step in range(4):
                            for tl in tls:
                                OP("vector", lambda e, tl=tl: e.scalar_tensor_tensor(
                                    U2S[tl % 2][:, :], US[tl][:, :], 0.9,
                                    YBS[tl][:, :], MULT, ADD))
                            for tl in tls:
                                OP("vector", lambda e, tl=tl: e.tensor_tensor(
                                    US[tl][:, :], U2S[tl % 2][:, :], SS[tl][:, :], SUB))
                            for tl in tls:
                                sw = [(cG, g_acc_last[tl])] if on_g else []
                                isgt_v[tl] = OP("vector", lambda e, tl=tl: e.tensor_single_scalar(
                                    SS[tl][:, :], US[tl][:, :], 0.0, ISGT),
                                    waits=sw, inc=(cV if on_g else None))
                            for tl in tls:
                                if on_g:
                                    g_acc_last[tl] = OP("gpsimd", lambda e, tl=tl, ai=ai: e.tensor_tensor(
                                        ACCS[tl][1 - ai][:, :], ACCS[tl][ai][:, :],
                                        SS[tl][:, :], ADD),
                                        waits=[(cV, isgt_v[tl])], inc=cG)
                                else:
                                    OP("vector", lambda e, tl=tl, ai=ai: e.tensor_tensor(
                                        ACCS[tl][1 - ai][:, :], ACCS[tl][ai][:, :],
                                        SS[tl][:, :], ADD))
                            ai = 1 - ai
                        einfo = {}
                        for tl in tls:
                            tb, l1, l2 = tile_loads[(su, eh, tg, tl)]
                            ob = n_out % 2
                            n_out += 1
                            einfo[tl] = (tb, ob)
                            ew = [(dXb[tb], l1), (dATb[tb], l2),
                                  (dOUb[ob], outst_free[ob])]
                            if pair == 0:
                                ew.append((cG, g_acc_last[tl]))
                            OP("vector", lambda e, tb=tb, tl=tl, ai=ai: e.scalar_tensor_tensor(
                                U2S[tl % 2][:, :], ACCS[tl][ai][:, :], P["fifth"][:, 0:1],
                                at2s[tb], MULT, ADD), waits=ew)
                        for tl in tls:
                            tb, ob = einfo[tl]
                            vout = OP("vector", lambda e, tb=tb, ob=ob, tl=tl: e.tensor_tensor(
                                outsts[ob], U2S[tl % 2][:, :], xt2s[tb], ADD),
                                inc=cV)
                            xt2_free[tb] = vout
                            at2_free[tb] = vout
                            u_free[tl] = vout
                            rows = su * SUPT + tg * 512 + tl * 128
                            cols = eh * 512
                            outst_free[ob] = OP("sync", lambda e, ob=ob, rows=rows,
                                                cols=cols: e.dma_start(
                                out_h.ap()[rows:rows + 128, cols:cols + 512],
                                outsts[ob]), waits=[(cV, vout)], inc=dOUb[ob])
                        if pair == 0:
                            issue_tile_load(su, eh, tg, 2)
                            issue_tile_load(su, eh, tg, 3)
        if DEBUG:
            OP("sync", lambda e: e.dma_start(dbg["dhsb"].ap(), hsb[:, :]),
               waits=[(cV, cV.v)], inc=dM)
        OP("sync", lambda e: None,
           waits=[(dOUb[0], dOUb[0].v), (dOUb[1], dOUb[1].v)])
        rec.flush(nc)

    return nc


_CACHED = {}


def _host_prep(L0_attn_WQ_w, L0_attn_WQ_b, L0_attn_WK_w, L0_attn_WK_b,
               L0_attn_WV_w, L0_attn_WV_b, L0_attn_Wo_w, L0_attn_Wo_b,
               L0_ffn_W_fc, L0_ffn_W_proj):
    f32 = np.float32
    WQ = np.asarray(L0_attn_WQ_w, f32)
    WK = np.asarray(L0_attn_WK_w, f32)
    WV = np.asarray(L0_attn_WV_w, f32)
    WO = np.asarray(L0_attn_Wo_w, f32)
    Wfc = np.asarray(L0_ffn_W_fc, f32)
    Wpj = np.asarray(L0_ffn_W_proj, f32)

    def qk_tile(W):
        t = W.reshape(8, 128, 8, 128).transpose(0, 3, 2, 1)
        return np.ascontiguousarray(t).reshape(D, D)

    def de_tile(W):
        t = W.reshape(D, 8, 128).transpose(2, 1, 0)
        return np.ascontiguousarray(t).reshape(128, 8 * D)

    wq = qk_tile(WQ)
    wk = qk_tile(WK)
    wv = de_tile(WV)
    wo = de_tile(WO)
    wfc = np.ascontiguousarray(
        Wfc.reshape(32, 128, 8, 128).transpose(0, 3, 2, 1)).reshape(FF, D)
    wpj = np.ascontiguousarray(
        Wpj.reshape(D, 32, 128).transpose(1, 2, 0)).reshape(FF, D)

    bq = np.ascontiguousarray(np.asarray(L0_attn_WQ_b, f32).reshape(8, 128).T)
    bk = np.ascontiguousarray(np.asarray(L0_attn_WK_b, f32).reshape(8, 128).T)
    bv = np.asarray(L0_attn_WV_b, f32).reshape(1, D)
    bo = np.asarray(L0_attn_Wo_b, f32).reshape(1, D)

    half = DK // 2
    i = np.arange(half, dtype=np.float64)
    freq = 1.0 / 10000.0 ** (2.0 * i / DK)
    tpos = np.arange(T, dtype=np.float64)
    ang = tpos[None, :] * freq[:, None]
    cos32 = np.cos(ang)
    sin32 = np.sin(ang)
    cosf = np.empty((128, T), np.float64)
    sinf = np.empty((128, T), np.float64)
    for blk in range(4):
        cosf[blk * 32:(blk + 1) * 32] = cos32
        sgn = -1.0 if blk % 2 == 0 else 1.0
        sinf[blk * 32:(blk + 1) * 32] = sgn * sin32
    cosf = cosf.astype(f32)
    sinf = sinf.astype(f32)

    cb = np.zeros((128, 4 * 128), bfloat16)
    cb[:, 0:128] = np.eye(128, dtype=f32).astype(bfloat16)
    sp = np.zeros((128, 128), f32)
    for p in range(128):
        base = (p // 64) * 64
        d = p % 64
        sp[base + (d + 32) % 64, p] = 1.0
    cb[:, 128:256] = sp.astype(bfloat16)
    tri = np.where(np.arange(128)[None, :] < np.arange(128)[:, None],
                   f32(-30000.0), f32(0.0))
    cb[:, 256:384] = tri.astype(bfloat16)
    om = np.zeros((128, 128), f32)
    for c in range(8):
        om[0:64, c * 16 + 2 * c] = 1.0
        om[64:128, c * 16 + 2 * c + 1] = 1.0
    cb[:, 384:512] = om.astype(bfloat16)
    onesrow = np.ones((1, 128), bfloat16)

    return dict(wq=wq, wk=wk, wv=wv, wo=wo, wfc=wfc, wpj=wpj,
                bqv=bq, bkv=bk, bvv=bv, bov=bo, cosf=cosf, sinf=sinf,
                cb=cb, onesrow=onesrow,
                cvals_in=np.broadcast_to(
                    np.array([1e-5, 2e-5, -1.0, -0.1, 0.125, 0.0, 1.0, 5.0], f32),
                    (128, 8)).copy())


def kernel(**inputs):
    x = np.ascontiguousarray(np.asarray(inputs["L0_attn_x"], np.float32))
    shared = _host_prep(
        inputs["L0_attn_WQ_w"], inputs["L0_attn_WQ_b"],
        inputs["L0_attn_WK_w"], inputs["L0_attn_WK_b"],
        inputs["L0_attn_WV_w"], inputs["L0_attn_WV_b"],
        inputs["L0_attn_Wo_w"], inputs["L0_attn_Wo_b"],
        inputs["L0_ffn_W_fc"], inputs["L0_ffn_W_proj"])
    nc = build_nc()
    in_maps = []
    for i in range(8):
        m = dict(shared)
        m["x"] = np.ascontiguousarray(x[i])
        in_maps.append(m)
    res = run_bass_kernel_spmd(nc, in_maps, core_ids=list(range(8)))
    return np.stack([np.asarray(res.results[i]["out"]) for i in range(8)], axis=0)


if __name__ == "__main__":
    nc = build_nc()
    print("built:", nc.n_instructions(), "instructions")


# version 31
# speedup vs baseline: 1.0593x; 1.0543x over previous
import numpy as np
import ml_dtypes
import concourse.bass as bass
import concourse.mybir as mybir
from concourse.bass_utils import run_bass_kernel_spmd
from concourse.alu_op_type import AluOpType

F32 = mybir.dt.float32
BF16 = mybir.dt.bfloat16
AF = mybir.ActivationFunctionType
AX = mybir.AxisListType.X
bfloat16 = ml_dtypes.bfloat16
MULT = AluOpType.mult
ADD = AluOpType.add
SUB = AluOpType.subtract
MAXOP = AluOpType.max
MINOP = AluOpType.min
ISGT = AluOpType.is_gt

D = 1024
T = 2048
H = 16
DK = 64
FF = 4096
NCH = 8
NTC = 16
VROW = H * 65
NSUP = 2
SUPT = 1024
DEBUG = False


class Ctr:
    def __init__(self, nc, name, step=1):
        self._cm = nc.semaphore(name)
        self.h = self._cm.__enter__()
        self.step = step
        self.v = 0


class Rec:
    def __init__(self):
        self.reset()

    def reset(self):
        self.ops = {"sync": [], "tensor": [], "scalar": [], "vector": [], "gpsimd": []}
        self.waited = {k: {} for k in self.ops}

    def op(self, eng, fn, waits=(), inc=None):
        w = []
        for (c, v) in waits:
            if v is None or v <= 0:
                continue
            key = id(c)
            if self.waited[eng].get(key, -1) >= v:
                continue
            self.waited[eng][key] = v
            w.append((c.h, v))
        if inc is not None:
            inc.v += inc.step

        def replay(engine, fn=fn, w=tuple(w), inc=inc):
            for (h, v) in w:
                engine.wait_ge(h, v)
            inst = fn(engine)
            if inc is not None:
                inst.then_inc(inc.h, inc.step)
        self.ops[eng].append(replay)
        return inc.v if inc is not None else None

    def flush(self, nc):
        if not any(self.ops.values()):
            return
        with nc.Block() as block:
            for name in ("sync", "tensor", "scalar", "vector", "gpsimd"):
                lst = self.ops[name]
                if lst:
                    def body(engine, lst=tuple(lst)):
                        for f in lst:
                            f(engine)
                    getattr(block, name)(body)
        self.reset()


def tap(t, part0, nparts, off, dims):
    W = int(t.shape[1])
    return bass.AP(t, part0 * W + off, [[W, nparts]] + [list(d) for d in dims])


def bcast_row(t, row, off, w, nrep):
    W = int(t.shape[1])
    return bass.AP(t, row * W + off, [[W, 1], [0, nrep], [1, w]])


def build_nc():
    nc = bass.Bass("TRN2", target_bir_lowering=False)

    x_h = nc.dram_tensor("x", [T, D], F32, kind="ExternalInput")
    wq_h = nc.dram_tensor("wq", [D, D], F32, kind="ExternalInput")
    wk_h = nc.dram_tensor("wk", [D, D], F32, kind="ExternalInput")
    wv_h = nc.dram_tensor("wv", [128, 8 * D], F32, kind="ExternalInput")
    wo_h = nc.dram_tensor("wo", [128, 8 * D], F32, kind="ExternalInput")
    wfc_h = nc.dram_tensor("wfc", [FF, D], F32, kind="ExternalInput")
    wpj_h = nc.dram_tensor("wpj", [FF, D], F32, kind="ExternalInput")
    bq_h = nc.dram_tensor("bqv", [128, 8], F32, kind="ExternalInput")
    bk_h = nc.dram_tensor("bkv", [128, 8], F32, kind="ExternalInput")
    bv_h = nc.dram_tensor("bvv", [1, D], F32, kind="ExternalInput")
    bo_h = nc.dram_tensor("bov", [1, D], F32, kind="ExternalInput")
    cosf_h = nc.dram_tensor("cosf", [128, T], F32, kind="ExternalInput")
    sinf_h = nc.dram_tensor("sinf", [128, T], F32, kind="ExternalInput")
    cb_h = nc.dram_tensor("cb", [128, 4 * 128], BF16, kind="ExternalInput")
    onesrow_h = nc.dram_tensor("onesrow", [1, 128], BF16, kind="ExternalInput")
    cvals_h = nc.dram_tensor("cvals_in", [128, 8], F32, kind="ExternalInput")
    out_h = nc.dram_tensor("out", [T, D], F32, kind="ExternalOutput")
    ascr_h = nc.dram_tensor("ascr", [T, D], BF16,
                            kind="ExternalOutput" if DEBUG else "Internal")
    if DEBUG:
        dbg = {
            "dxnT": nc.dram_tensor("dxnT", [128, NCH * T], BF16, kind="ExternalOutput"),
            "dvsb": nc.dram_tensor("dvsb", [128, NTC * VROW], BF16, kind="ExternalOutput"),
            "dqr0": nc.dram_tensor("dqr0", [128, 4 * T], BF16, kind="ExternalOutput"),
            "dkr0": nc.dram_tensor("dkr0", [128, 4 * T], BF16, kind="ExternalOutput"),
            "dqr1": nc.dram_tensor("dqr1", [128, 4 * T], BF16, kind="ExternalOutput"),
            "dkr1": nc.dram_tensor("dkr1", [128, 4 * T], BF16, kind="ExternalOutput"),
            "dcrep": nc.dram_tensor("dcrep", [128, 64], F32, kind="ExternalOutput"),
            "dLbuf": nc.dram_tensor("dLbuf", [64, 512], F32, kind="ExternalOutput"),
            "dRbuf": nc.dram_tensor("dRbuf", [64, 512], F32, kind="ExternalOutput"),
            "dosb": nc.dram_tensor("dosb", [128, NCH * T], BF16, kind="ExternalOutput"),
            "dhsb": nc.dram_tensor("dhsb", [128, 32 * SUPT], BF16, kind="ExternalOutput"),
            "dqn": nc.dram_tensor("dqn", [16, T], BF16, kind="ExternalOutput"),
            "dkn": nc.dram_tensor("dkn", [16, T], BF16, kind="ExternalOutput"),
            "dmvh": nc.dram_tensor("dmvh", [128, 64], F32, kind="ExternalOutput"),
            "dosbr": nc.dram_tensor("dosbr", [128, NCH * T], BF16, kind="ExternalOutput"),
            "dcosb0": nc.dram_tensor("dcosb0", [128, T], BF16, kind="ExternalOutput"),
            "dcosb1": nc.dram_tensor("dcosb1", [128, T], BF16, kind="ExternalOutput"),
            "drr0": nc.dram_tensor("drr0", [128, T], BF16, kind="ExternalOutput"),
            "drr1": nc.dram_tensor("drr1", [128, T], BF16, kind="ExternalOutput"),
            "dpool": nc.dram_tensor("dpool", [128, 32], F32, kind="ExternalOutput"),
            "dmv": nc.dram_tensor("dmv", [128, 32], F32, kind="ExternalOutput"),
            "dsqv": nc.dram_tensor("dsqv", [128, 16], F32, kind="ExternalOutput"),
            "dmsum": nc.dram_tensor("dmsum", [128, 16], F32, kind="ExternalOutput"),
            "drstd": nc.dram_tensor("drstd", [128, 16], F32, kind="ExternalOutput"),
            "dnegms": nc.dram_tensor("dnegms", [128, 16], F32, kind="ExternalOutput"),
        }

    dM = Ctr(nc, "dM", 16)
    dG = Ctr(nc, "dG", 16)
    dXb = [Ctr(nc, "dX0", 16), Ctr(nc, "dX1", 16)]
    dATb = [Ctr(nc, "dAT0", 16), Ctr(nc, "dAT1", 16)]
    dASb = [Ctr(nc, "dAS%d" % i_, 16) for i_ in range(6)]
    dLb = [Ctr(nc, "dL0", 16), Ctr(nc, "dL1", 16)]
    dOSTb = [Ctr(nc, "dOST0", 16), Ctr(nc, "dOST1", 16)]
    dWQb = [Ctr(nc, "dWQ0", 16), Ctr(nc, "dWQ1", 16)]
    dWFb = [Ctr(nc, "dWF0", 16), Ctr(nc, "dWF1", 16)]
    dRRb = [Ctr(nc, "dRR0", 16), Ctr(nc, "dRR1", 16)]
    dWBb = [Ctr(nc, "dWB0", 16), Ctr(nc, "dWB1", 16)]
    dOUb = [Ctr(nc, "dOU0", 16), Ctr(nc, "dOU1", 16)]
    cT = Ctr(nc, "cT")
    cS = Ctr(nc, "cS")
    cV = Ctr(nc, "cV")

    rec = Rec()
    OP = rec.op

    def alloc(name, shape, dtype, side=None):
        if side is None:
            cm = nc.sbuf_tensor(name, shape, dtype)
        else:
            cm = nc.sbuf_tensor(name, shape, dtype, side=side)
        return cm, cm.__enter__()

    P = {}
    cms = []
    for name, shape, dt_ in [
        ("xnT", [128, NCH * T], BF16),
        ("xt0", [128, D], F32), ("xt1", [128, D], F32),
        ("xn0", [128, D], BF16), ("xn1", [128, D], BF16),
        ("bnout", [128, 12], F32), ("mv", [128, 32], F32),
        ("mvh", [128, 64], F32), ("msum", [128, 16], F32),
        ("pool", [128, 32], F32), ("rstdf", [128, 16], F32),
        ("sqv", [128, 16], F32), ("rstd", [128, 16], F32), ("negms", [128, 16], F32),
        ("bq_sb", [128, 8], F32), ("bk_sb", [128, 8], F32),
        ("bvb", [1, D], BF16), ("bob", [1, D], BF16),
        ("cbc", [128, 4 * 128], BF16), ("onesr", [1, 128], BF16),
        ("Lbuf", [64, 512], F32), ("Rbuf", [64, 512], F32),
        ("crep", [128, 64], F32), ("negcf", [1, 64], F32),
        ("five", [128, 1], F32), ("fifth", [128, 1], F32),
        ("cvals", [128, 8], F32),
    ]:
        cm, t = alloc(name, shape, dt_)
        cms.append(cm)
        P[name] = t
    xnT = P["xnT"]
    xts = [P["xt0"], P["xt1"]]
    xns = [P["xn0"], P["xn1"]]
    cbc = P["cbc"]

    def ident():
        return cbc[:, 0:128]

    def sperm():
        return cbc[:, 128:256]

    def trimask():
        return cbc[:, 256:384]

    def onesm(c):
        return cbc[:, 384 + c * 16:384 + (c + 1) * 16]

    CVALS = [1e-5, 2e-5, -1.0, -0.1, 0.125, 0.0, 1.0, 5.0]
    for i, val in enumerate(CVALS):
        nc.const_aps.register(F32, val, P["cvals"][:, i:i + 1])

    osb24_cm, osb24 = alloc("osb24", [128, T], BF16)
    osb25_cm, osb25 = alloc("osb25", [128, T], BF16)
    osb_cm, osb = alloc("osb", [128, NCH * T], BF16)
    wot_cm, wot = alloc("wot", [128, 8 * D], BF16)
    vsb_cm, vsb = alloc("vsb", [128, NTC * VROW], BF16)
    wvt_cm, wvt = alloc("wvt", [128, 8 * D], BF16, side="right")

    OP("sync", lambda e: e.dma_start(P["bq_sb"][:, :], bq_h.ap()), inc=dM)
    OP("sync", lambda e: e.dma_start(P["bk_sb"][:, :], bk_h.ap()), inc=dM)
    OP("sync", lambda e: e.dma_start(cbc[:, :], cb_h.ap()), inc=dM)
    cl0 = OP("sync", lambda e: e.dma_start(P["onesr"][:, :], onesrow_h.ap()), inc=dM)
    cvld = OP("sync", lambda e: e.dma_start(P["cvals"][:, :], cvals_h.ap()), inc=dM)
    OP("gpsimd", lambda e: e.dma_start(P["bvb"][:, :], bv_h.ap()), inc=dG)
    OP("gpsimd", lambda e: e.dma_start(P["bob"][:, :], bo_h.ap()), inc=dG)
    wvt_ld = OP("gpsimd", lambda e: e.dma_start(wvt[:, :], wv_h.ap()), inc=dG)
    wot_ld = OP("gpsimd", lambda e: e.dma_start(wot[:, :], wo_h.ap()), inc=dG)

    with nc.psum_tensor("psT0", [128, 512], F32) as psT0, \
         nc.psum_tensor("psT1", [128, 512], F32) as psT1, \
         nc.psum_tensor("psV0", [128, 1024], F32) as psV0, \
         nc.psum_tensor("psV1", [128, 1024], F32) as psV1:
        psT = [psT0, psT1]
        psV = [psV0, psV1]
        ones_ap = bass.AP(vsb, 64, [[NTC * VROW, 128], [VROW, NTC], [65, H], [1, 1]])
        OP("vector", lambda e: e.memset(P["five"][:, :], 5.0))
        OP("vector", lambda e: e.memset(ones_ap, 1.0))
        evacT = {0: None, 1: None}
        evacV = {0: None, 1: None}
        xn_done = [None] * NTC
        sqd_l = [None] * NTC

        def stage_A(t):
            w = [(cS, xn_done[t - 2])] if t >= 2 else []
            ld = OP("sync", lambda e, t=t: e.dma_start(
                xts[t % 2][:, :], x_h.ap()[t * 128:(t + 1) * 128, :]), waits=w,
                inc=dXb[t % 2])
            OP("vector", lambda e, t=t: e.bn_stats(
                P["bnout"][:, 0:6], xts[t % 2][:, 0:512]), waits=[(dXb[t % 2], ld)])
            OP("vector", lambda e, t=t: e.bn_stats(
                P["bnout"][:, 6:12], xts[t % 2][:, 512:1024]))
            OP("vector", lambda e, t=t: e.bn_aggr(
                P["mvh"][:, 4 * t:4 * t + 2], P["bnout"][:, 0:6]))
            ag2 = OP("vector", lambda e, t=t: e.bn_aggr(
                P["mvh"][:, 4 * t + 2:4 * t + 4], P["bnout"][:, 6:12]), inc=cV)
            mc = 4 * t
            md = OP("vector", lambda e, mc=mc, t=t: e.tensor_tensor(
                P["mv"][:, t:t + 1], P["mvh"][:, mc:mc + 1],
                P["mvh"][:, mc + 2:mc + 3], SUB), waits=[(cV, ag2)], inc=cV)
            OP("vector", lambda e, mc=mc, t=t: e.tensor_tensor(
                P["msum"][:, t:t + 1], P["mvh"][:, mc:mc + 1],
                P["mvh"][:, mc + 2:mc + 3], ADD))
            msq = OP("vector", lambda e, t=t: e.tensor_tensor(
                P["pool"][:, t:t + 1], P["mv"][:, t:t + 1], P["mv"][:, t:t + 1],
                MULT), waits=[(cV, md)], inc=cV)
            cd = OP("vector", lambda e, mc=mc, t=t: e.scalar_tensor_tensor(
                P["pool"][:, 16 + t:17 + t], P["pool"][:, t:t + 1], 0.5,
                P["mvh"][:, mc + 1:mc + 2], MULT, ADD), waits=[(cV, msq)], inc=cV)
            vd = OP("vector", lambda e, mc=mc, t=t: e.tensor_tensor(
                P["mv"][:, 16 + t:17 + t], P["pool"][:, 16 + t:17 + t],
                P["mvh"][:, mc + 3:mc + 4], ADD), waits=[(cV, cd)], inc=cV)
            sqd_l[t] = OP("scalar", lambda e, t=t: e.activation(
                P["sqv"][:, t:t + 1], P["mv"][:, 16 + t:17 + t], AF.Sqrt,
                bias=2e-5), waits=[(cV, vd), (dM, cvld)], inc=cS)

        def stage_B(t):
            rr = OP("vector", lambda e, t=t: e.reciprocal(
                P["rstd"][:, t:t + 1], P["sqv"][:, t:t + 1]),
                waits=[(cS, sqd_l[t])], inc=cV)
            OP("vector", lambda e, t=t: e.scalar_tensor_tensor(
                P["negms"][:, t:t + 1], P["msum"][:, t:t + 1],
                -0.7071067811865476,
                P["rstd"][:, t:t + 1], MULT, MULT), waits=[(cV, rr)])
            sf = OP("vector", lambda e, t=t: e.tensor_single_scalar(
                P["rstdf"][:, t:t + 1], P["rstd"][:, t:t + 1],
                1.4142135623730951, MULT), inc=cV)
            xn_done[t] = OP("scalar", lambda e, t=t: e.activation(
                xns[t % 2][:, :], xts[t % 2][:, :], AF.Identity,
                bias=P["negms"][:, t:t + 1], scale=P["rstdf"][:, t:t + 1]),
                waits=[(cV, sf)], inc=cS)

        def stage_TP(t):
            for cg in range(2):
                bank = (t * 2 + cg) % 2
                mmw = [(cS, xn_done[t]), (cS, evacT[bank])]
                last = None
                for c4 in range(4):
                    c = cg * 4 + c4
                    last = OP("tensor", lambda e, t=t, c=c, c4=c4, bank=bank: e.matmul(
                        psT[bank][:, c4 * 128:(c4 + 1) * 128],
                        xns[t % 2][:, c * 128:(c + 1) * 128],
                        ident(), start=True, stop=True),
                        waits=(mmw if c4 == 0 else []),
                        inc=(cT if c4 == 3 else None))
                dst = tap(xnT, 0, 128, t * 128 + (cg * 4) * T, [[T, 4], [1, 128]])
                evacT[bank] = OP("scalar", lambda e, bank=bank, dst=dst: e.activation(
                    dst, psT[bank][:, :], AF.Copy), waits=[(cT, last)], inc=cS)
            tc = t
            bank = tc % 2
            mmw = [(dG, wvt_ld), (cS, evacV[bank]), (cS, evacT[(t * 2 + 1) % 2])]
            last = None
            for sch in range(NCH):
                for nh in range(2):
                    last = OP("tensor", lambda e, tc=tc, sch=sch, nh=nh, bank=bank: e.matmul(
                        psV[bank][:, nh * 512:(nh + 1) * 512],
                        xnT[:, sch * T + tc * 128: sch * T + (tc + 1) * 128],
                        wvt[:, sch * D + nh * 512: sch * D + (nh + 1) * 512],
                        start=(sch == 0), stop=False),
                        waits=(mmw if (sch == 0 and nh == 0) else []))
            for nh in range(2):
                last = OP("tensor", lambda e, nh=nh, bank=bank: e.matmul(
                    psV[bank][:, nh * 512:(nh + 1) * 512],
                    P["onesr"][0:1, 0:128], P["bvb"][0:1, nh * 512:(nh + 1) * 512],
                    start=False, stop=True), inc=(cT if nh == 1 else None))
            dstv = tap(vsb, 0, 128, tc * VROW, [[65, H], [1, 64]])
            evacV[bank] = OP("scalar", lambda e, dstv=dstv, bank=bank: e.activation(
                dstv, psV[bank][:, :], AF.Copy), waits=[(cT, last)], inc=cS)

        for t in range(NTC + 1):
            if t < NTC:
                stage_A(t)
            if t >= 1:
                stage_B(t - 1)
                stage_TP(t - 1)
        OP("vector", lambda e: e.reciprocal(P["fifth"][:, :], P["five"][:, :]))
        if DEBUG:
            OP("sync", lambda e: e.dma_start(dbg["dvsb"].ap(), vsb[:, :]),
               waits=[(cS, cS.v), (cV, cV.v)], inc=dM)
            OP("sync", lambda e: e.dma_start(dbg["dxnT"].ap(), xnT[:, :]), inc=dM)
        rec.flush(nc)
    wvt_cm.__exit__(None, None, None)

    p2_named = {}
    p2_cms = []
    for name, shape, dt_ in [
        ("qr", [128, 4 * T], BF16), ("kr", [128, 4 * T], BF16),
        ("qraw0", [128, T], BF16), ("qraw1", [128, T], BF16),
        ("qsq0", [128, 512], BF16), ("qsq1", [128, 512], BF16),
        ("sqs0", [128, 512], BF16), ("sqs1", [128, 512], BF16),
        ("rtmp0", [128, 512], BF16), ("rtmp1", [128, 512], BF16),
        ("rtmp20", [128, 512], BF16), ("rtmp21", [128, 512], BF16),
        ("cosb", [128, T], BF16), ("sinb", [128, T], BF16),
        ("pt0", [128, 1024], BF16), ("pt1", [128, 1024], BF16),
        ("pt2", [128, 1024], BF16),
        ("qnt", [16, 4], F32), ("qnt2", [16, 4], F32), ("knm", [16, 1], F32),
    ]:
        cm, t = alloc(name, shape, dt_)
        p2_cms.append(cm)
        p2_named[name] = t
    G = p2_named
    qraws = [G["qraw0"], G["qraw1"]]
    qsqs = [G["qsq0"], G["qsq1"]]
    sqss = [G["sqs0"], G["sqs1"]]
    rtmps = [G["rtmp0"], G["rtmp1"]]
    rtmps2 = [G["rtmp20"], G["rtmp21"]]
    pts = [G["pt0"], G["pt1"], G["pt2"]]
    ostg = P["xt1"].bitcast(BF16)
    lsts = [P["xt0"][:, 0:512], P["xt0"][:, 512:1024]]
    wqks = [P["xn0"], P["xn1"]]
    ct_p01 = cT.v

    OP("gpsimd", lambda e: e.dma_start(G["cosb"][:, :], cosf_h.ap()), inc=dG)
    tabs_ld = OP("gpsimd", lambda e: e.dma_start(G["sinb"][:, :], sinf_h.ap()), inc=dG)

    lrec_dma = [None, None]
    norm_state = {"mult_prev": [None, None], "wb_prev": [None, None]}

    def emit_norm_chunk(c, rrep_t, osb2_t, osb_t, recip_cv, extra_first=(),
                        writeback=True):
        rb = c % 2
        first_w = [(cV, recip_cv),
                   (cV, norm_state["mult_prev"][rb])] + list(extra_first)
        gsrc = bass.AP(P["Rbuf"], 8 * c * 512, [[512, 8], [1, 512]])
        gdst = bass.AP(osb2_t, 0, [[2048, 2], [1, 2048]])
        gat = OP("gpsimd", lambda e, gsrc=gsrc, gdst=gdst: e.dma_start(gdst, gsrc),
                 waits=first_w + [(dWBb[rb], norm_state["wb_prev"][rb])],
                 inc=dWQb[rb])
        bc0 = OP("gpsimd", lambda e, osb2_t=osb2_t, rrep_t=rrep_t: e.dma_start(
            tap(rrep_t, 0, 64, 0, [[1, 2048]]),
            bcast_row(osb2_t, 0, 0, 2048, 64)),
            waits=[(dWQb[rb], gat)], inc=dRRb[rb])
        bc1 = OP("gpsimd", lambda e, osb2_t=osb2_t, rrep_t=rrep_t: e.dma_start(
            tap(rrep_t, 64, 64, 0, [[1, 2048]]),
            bcast_row(osb2_t, 1, 0, 2048, 64)), inc=dWQb[rb])
        last_dma = bc0
        mw = [(dRRb[rb], bc0), (dWQb[rb], bc1)]
        mv_ = OP("vector", lambda e, c=c, rrep_t=rrep_t, osb2_t=osb2_t,
                 osb_t=osb_t: e.tensor_tensor(
            osb2_t[:, :], osb_t[:, c * T:(c + 1) * T], rrep_t[:, :], MULT),
            waits=mw, inc=cV)
        norm_state["mult_prev"][rb] = mv_
        norm_state["mult_chunk_%d" % c] = mv_
        if writeback:
            norm_state["wb_prev"][rb] = OP("sync", lambda e, c=c, osb2_t=osb2_t,
                                           osb_t=osb_t: e.dma_start(
                osb_t[:, c * T:(c + 1) * T], osb2_t[:, :]),
                waits=[(cV, mv_)], inc=dWBb[rb])
            norm_state["wb_chunk_%d" % c] = norm_state["wb_prev"][rb]
    recip_done = [None, None]
    sc_free = {0: None, 1: None, 2: None}
    pt_free = {0: None, 1: None, 2: None}
    av_free = {0: None, 1: None}
    lst_free = {0: None, 1: None}
    ost_free = {0: None, 1: None}
    crep_ready = [None, None]
    prev_wmm = [None, None]

    for g in range(2):
        with nc.psum_tensor(f"psP0_{g}", [128, 512], F32) as psP0, \
             nc.psum_tensor(f"psP1_{g}", [128, 512], F32) as psP1, \
             nc.psum_tensor(f"psR0_{g}", [128, 512], F32) as psR0, \
             nc.psum_tensor(f"psR1_{g}", [128, 512], F32) as psR1, \
             nc.psum_tensor(f"psN0_{g}", [16, 512], F32) as psN0, \
             nc.psum_tensor(f"psN1_{g}", [16, 512], F32) as psN1, \
             nc.psum_tensor(f"psN2_{g}", [16, 512], F32) as psN2, \
             nc.psum_tensor(f"psN3_{g}", [16, 512], F32) as psN3:
            psP = [psP0, psP1]
            psR = [psR0, psR1]
            psN = [psN0, psN1, psN2, psN3]

            if g == 1:
                recip_done[0] = OP("vector", lambda e: e.reciprocal(
                    P["Rbuf"][0:32, :], P["Lbuf"][0:32, :]),
                    waits=[(dLb[0], lrec_dma[0][0]), (dLb[1], lrec_dma[0][1])], inc=cV)

            sqev = {}
            normmm = {}
            ropeV = {}
            swapev = {}
            ropemul = {}
            psn_red = {"q": [None] * 4, "k": [None] * 4}
            wmm = {}
            pending_ns = None
            for ch in range(8):
                qk = "q" if ch < 4 else "k"
                cl = ch % 4
                cg = 4 * g + cl
                wh = wq_h if qk == "q" else wk_h
                dest = G["qr"] if qk == "q" else G["kr"]
                bias_sb = P["bq_sb"] if qk == "q" else P["bk_sb"]
                if ch >= 2:
                    wwaits = [(cT, wmm[ch - 2])]
                elif g == 0:
                    wwaits = [(cT, ct_p01)]
                else:
                    wwaits = [(cT, prev_wmm[ch])]
                wdma = OP("gpsimd", lambda e, wh=wh, cg=cg, ch=ch: e.dma_start(
                    wqks[ch % 2][:, :], wh.ap()[cg * 128:(cg + 1) * 128, :]),
                    waits=wwaits, inc=dWQb[ch % 2])
                for st in range(4):
                    idx = ch * 4 + st
                    bank = st % 2
                    mmw = [(dWQb[ch % 2], wdma)]
                    if idx >= 2:
                        mmw.append((cS, sqev[idx - 2]))
                    last = None
                    for s in range(NCH):
                        last = OP("tensor", lambda e, ch=ch, st=st, s=s, bank=bank: e.matmul(
                            psP[bank][:, 0:512],
                            wqks[ch % 2][:, s * 128:(s + 1) * 128],
                            xnT[:, s * T + st * 512: s * T + (st + 1) * 512],
                            start=(s == 0), stop=(s == NCH - 1)),
                            waits=(mmw if s == 0 else []),
                            inc=(cT if s == NCH - 1 else None))
                    if st == 3:
                        wmm[ch] = last
                    cwaits = [(cT, last)]
                    if ch >= 2 and st == 0:
                        cwaits.append((cV, ropeV[ch - 2]))
                    cpd = OP("scalar", lambda e, ch=ch, st=st, bank=bank,
                             bias_sb=bias_sb, cg=cg: e.activation(
                        qraws[ch % 2][:, st * 512:(st + 1) * 512],
                        psP[bank][:, 0:512], AF.Identity,
                        bias=bias_sb[:, cg:cg + 1]), waits=cwaits, inc=cS)
                    sqw = []
                    if idx >= 2:
                        sqw.append((cT, normmm[idx - 2]))
                    sqev[idx] = OP("scalar", lambda e, ch=ch, st=st, bank=bank,
                                   bias_sb=bias_sb, cg=cg: e.activation(
                        qsqs[st % 2][:, :], psP[bank][:, 0:512], AF.Square,
                        bias=bias_sb[:, cg:cg + 1]), waits=sqw, inc=cS)
                    def emit_normswap(idx=idx, ch=ch, st=st, cl=cl, cg=cg, qk=qk,
                                      dest=dest, cpd=cpd):
                        nw = [(cS, sqev[idx])]
                        if ch == 4:
                            nw.append((cV, psn_red["q"][st]))
                        normmm[idx] = OP("tensor", lambda e, st=st, cg=cg, cl=cl: e.matmul(
                            psN[st][0:16, 0:512], onesm(cg), qsqs[st % 2][:, :],
                            start=(cl == 0), stop=(cl == 3)), waits=nw, inc=cT)
                        sww = [(cS, cpd)]
                        if idx >= 2:
                            sww.append((cS, swapev[idx - 2]))
                        swmm = OP("tensor", lambda e, ch=ch, st=st: e.matmul(
                            psR[st % 2][:, 0:512], sperm(),
                            qraws[ch % 2][:, st * 512:(st + 1) * 512],
                            start=True, stop=True), waits=sww, inc=cT)
                        sew = [(cT, swmm)]
                        if idx >= 2:
                            sew.append((cV, ropemul[idx - 2]))
                        swapev[idx] = OP("scalar", lambda e, st=st: e.activation(
                            sqss[st % 2][:, :], psR[st % 2][:, 0:512], AF.Copy),
                            waits=sew, inc=cS)
                        vw = [(cS, swapev[idx])]
                        if g == 0 and ch == 0 and st == 0:
                            vw.append((dG, tabs_ld))
                        ropemul[idx] = OP("vector", lambda e, st=st: e.tensor_tensor(
                            rtmps[st % 2][:, :], sqss[st % 2][:, :],
                            G["sinb"][:, st * 512:(st + 1) * 512], MULT),
                            waits=vw, inc=cV)
                        r2v = OP("vector", lambda e, ch=ch, st=st: e.tensor_tensor(
                            rtmps2[st % 2][:, :],
                            qraws[ch % 2][:, st * 512:(st + 1) * 512],
                            G["cosb"][:, st * 512:(st + 1) * 512], MULT), inc=cV)
                        rv = OP("vector", lambda e, st=st, cl=cl, dest=dest: e.tensor_tensor(
                            dest[:, cl * T + st * 512: cl * T + (st + 1) * 512],
                            rtmps[st % 2][:, :], rtmps2[st % 2][:, :], ADD),
                            waits=[(cV, r2v)], inc=cV)
                        if st == 3:
                            ropeV[ch] = rv
                    if pending_ns is not None:
                        pending_ns()
                    pending_ns = emit_normswap
                if cl == 3:
                    if pending_ns is not None:
                        pending_ns()
                        pending_ns = None
                    dstt = G["qnt"] if qk == "q" else G["qnt2"]
                    for st in range(4):
                        psn_red[qk][st] = OP("vector", lambda e, st=st, dstt=dstt: e.tensor_reduce(
                            dstt[0:16, st:st + 1], psN[st][0:16, 0:512],
                            AX, MAXOP), waits=[(cT, normmm[ch * 4 + st])], inc=cV)

            prev_wmm = [wmm[6], wmm[7]]
            r0 = 8 * g
            knmv = OP("vector", lambda e: e.tensor_reduce(
                G["knm"][0:16, 0:1], G["qnt2"][0:16, 0:4], AX, MAXOP),
                waits=[(cV, psn_red["k"][3])], inc=cV)
            ncv = OP("vector", lambda e: e.tensor_scalar(
                G["qnt2"][0:16, 0:4], G["qnt"][0:16, 0:4],
                G["knm"][0:16, 0:1], -0.0625, ADD, MULT),
                waits=[(cV, knmv), (cV, psn_red["q"][3])], inc=cV)
            src = bass.AP(G["qnt2"], r0 * 4, [[4, 8], [1, 4]])
            d1 = OP("sync", lambda e, src=src, g=g: e.dma_start(
                bass.AP(P["negcf"], 32 * g, [[64, 1], [1, 32]]), src),
                waits=[(cV, ncv)], inc=dM)
            crep_ready[g] = OP("sync", lambda e, g=g: e.dma_start(
                tap(P["crep"], 0, 128, 32 * g, [[1, 32]]),
                bcast_row(P["negcf"], 0, 32 * g, 32, 128)),
                waits=[(dM, d1)], inc=dM)
            if DEBUG:
                OP("sync", lambda e, g=g: e.dma_start(dbg["dqr%d" % g].ap(), G["qr"][:, :]),
                   waits=[(cS, cS.v), (cV, cV.v), (dM, dM.v)], inc=dM)
                OP("sync", lambda e, g=g: e.dma_start(dbg["dkr%d" % g].ap(), G["kr"][:, :]), inc=dM)
            rec.flush(nc)

        with nc.psum_tensor(f"psS0_{g}", [128, 1024], F32) as psS0, \
             nc.psum_tensor(f"psS1_{g}", [128, 1024], F32) as psS1, \
             nc.psum_tensor(f"psS2_{g}", [128, 1024], F32) as psS2, \
             nc.psum_tensor(f"psA0_{g}", [128, 512], F32) as psA0, \
             nc.psum_tensor(f"psA1_{g}", [128, 512], F32) as psA1:
            psSc = [psS0, psS1, psS2]
            psAV = [psA0, psA1]

            def warm(n):
                pass
            nodd = 0
            if g == 1:
                norm_ex = [(dOSTb[0], dOSTb[0].v), (dOSTb[1], dOSTb[1].v), (cV, cV.v)]
            recip_half = [None]
            for hh in range(8):
                if g == 1 and 2 <= hh < 6:
                    c_ = hh - 2
                    emit_norm_chunk(c_, qraws[c_ % 2],
                                    [G["cosb"], G["sinb"]][c_ % 2], osb,
                                    recip_done[0],
                                    extra_first=(norm_ex if c_ == 0 else ()))
                if g == 1 and hh == 4:
                    recip_half[0] = OP("vector", lambda e: e.reciprocal(
                        P["Rbuf"][32:48, :], P["Lbuf"][32:48, :]),
                        waits=[(dLb[0], dLb[0].v), (dLb[1], dLb[1].v)], inc=cV)
                if g == 1 and hh in (6, 7):
                    c_ = hh - 2
                    emit_norm_chunk(c_, qraws[c_ % 2],
                                    [osb24, osb25][c_ % 2], osb,
                                    recip_half[0],
                                    extra_first=[(dOSTb[0], dOSTb[0].v),
                                                 (dOSTb[1], dOSTb[1].v)],
                                    writeback=False)
                h = 8 * g + hh
                cglob = h // 2
                cl = hh // 2
                off = (h % 2) * 64
                for qt in range(4):
                    qs = qt * 512
                    kmax = qt * 4 + 3
                    avb = (h * 4 + qt) % 2
                    exp_done = {}
                    last_av = None

                    def emit_sc_pair(pa, qt=qt, qs=qs, cl=cl, off=off, h=h, g=g):
                        b = pa % 3
                        scw = [(cS, sc_free[b])]
                        col0s = []
                        for half in range(2):
                            ki = 2 * pa + half
                            col0 = max(0, ki * 128 - qs)
                            col0s.append(col0)
                            diag = (ki * 128 >= qs)
                            OP("tensor", lambda e, cl=cl, off=off, ki=ki, qs=qs,
                               col0=col0, b=b, diag=diag, half=half: e.matmul(
                                psSc[b][:, half * 512 + col0: half * 512 + 512],
                                G["kr"][off:off + 64, cl * T + ki * 128: cl * T + (ki + 1) * 128],
                                G["qr"][off:off + 64, cl * T + qs + col0: cl * T + qs + 512],
                                start=True, stop=not diag),
                                waits=(scw if half == 0 else []),
                                inc=(None if diag else (cT if half == 1 else None)))
                            if diag:
                                OP("tensor", lambda e, b=b, col0=col0, half=half: e.matmul(
                                    psSc[b][:, half * 512 + col0: half * 512 + col0 + 128],
                                    ident(), trimask(),
                                    start=False, stop=True),
                                    inc=(cT if half == 1 else None))
                        sc = cT.v
                        expw = [(cT, sc), (cT, pt_free[b]), (dM, crep_ready[g])]
                        exp_done[pa] = OP("scalar", lambda e, b=b, c0=col0s[0], h=h,
                                          qt=qt: e.activation(
                            pts[b][:, c0:1024], psSc[b][:, c0:1024], AF.Exp,
                            bias=P["crep"][:, 4 * h + qt: 4 * h + qt + 1],
                            scale=0.125), waits=expw, inc=cS)
                        sc_free[b] = exp_done[pa]

                    def emit_av(ki, qt=qt, qs=qs, h=h, avb=avb, kmax=kmax):
                        nonlocal last_av
                        col0 = max(0, ki * 128 - qs)
                        pa = ki // 2
                        b = pa % 3
                        off2 = (ki % 2) * 512
                        avw = [(cS, exp_done[pa])]
                        if ki == 0:
                            avw.append((cV, av_free[avb]))
                        last_av = OP("tensor", lambda e, h=h, ki=ki, col0=col0, b=b,
                                     off2=off2, avb=avb, kmax=kmax: e.matmul(
                            psAV[avb][0:65, col0:512],
                            vsb[:, ki * VROW + h * 65: ki * VROW + h * 65 + 65],
                            pts[b][:, off2 + col0: off2 + 512],
                            start=(ki == 0), stop=(ki == kmax)), waits=avw, inc=cT)
                        if ki % 2 == 1:
                            pt_free[b] = last_av

                    npair = (kmax + 1) // 2
                    for pa in range(npair):
                        emit_sc_pair(pa)
                        warm(2)
                        if pa >= 2:
                            emit_av(2 * pa - 4)
                            emit_av(2 * pa - 3)
                    for ki in range(max(0, 2 * npair - 4), 2 * npair):
                        emit_av(ki)
                        warm(1)
                    lp = (h * 4 + qt) % 2
                    lcw = [(cT, last_av), (dLb[lp], lst_free[lp])]
                    lcp = OP("vector", lambda e, avb=avb, lp=lp: e.tensor_copy(
                        tap(P["xt0"], 64, 1, lp * 512, [[1, 512]]),
                        psAV[avb][64:65, 0:512]), waits=lcw, inc=cV)
                    if h % 2 == 0:
                        ocp = OP("vector", lambda e, avb=avb, cglob=cglob, qt=qt: e.tensor_copy(
                            osb[0:64, cglob * T + qt * 512: cglob * T + (qt + 1) * 512],
                            psAV[avb][0:64, 0:512]), inc=cV)
                    else:
                        ocw = [(dOSTb[0], ost_free[0])] if qt == 0 else []
                        ocp = OP("vector", lambda e, avb=avb, qt=qt: e.tensor_copy(
                            ostg[0:64, qt * 512:(qt + 1) * 512],
                            psAV[avb][0:64, 0:512]), waits=ocw, inc=cV)
                        if qt == 3:
                            ost_free[0] = OP("sync", lambda e, cglob=cglob: e.dma_start(
                                tap(osb, 64, 64, cglob * T, [[1, 2048]]),
                                ostg[0:64, 0:2048]), waits=[(cV, ocp)], inc=dOSTb[0])
                    av_free[avb] = ocp
                    lst_free[lp] = OP("sync", lambda e, lp=lp, h=h, qt=qt: e.dma_start(
                        bass.AP(P["Lbuf"], (4 * h + qt) * 512, [[512, 1], [1, 512]]),
                        tap(P["xt0"], 64, 1, lp * 512, [[1, 512]])),
                        waits=[(cV, lcp)], inc=dLb[lp])
            lrec_dma[g] = (dLb[0].v, dLb[1].v)
            rec.flush(nc)

    cs_drain = cS.v
    cv_drain = cV.v
    for cm in reversed(p2_cms):
        cm.__exit__(None, None, None)
    vsb_cm.__exit__(None, None, None)
    rrep2_cm, rrep2 = alloc("rrep2", [128, T], BF16)
    rrep3_cm, rrep3 = alloc("rrep3", [128, T], BF16)
    osb22_cm, osb22 = alloc("osb22", [128, T], BF16)
    osb23_cm, osb23 = alloc("osb23", [128, T], BF16)
    ast_cms = []
    asts = []
    for i_ in range(6):
        cm_, t_ = alloc("ast%d" % i_, [128, D], BF16)
        ast_cms.append(cm_)
        asts.append(t_)

    wpj_cm, wpj_sb = alloc("wpjsb", [128, 32 * D], BF16, side="right")
    wfcs_cm0, wfcs0 = alloc("wfcs0", [128, 1024], BF16, side="right")
    wfcs_cm1, wfcs1 = alloc("wfcs1", [128, 1024], BF16, side="right")
    wfc_pre = {}
    with nc.psum_tensor("psW0", [128, 1024], F32) as psW0, \
         nc.psum_tensor("psW1", [128, 1024], F32) as psW1:
        psW = [psW0, psW1]
        for fc0 in range(2):
            wfc_pre[fc0] = OP("gpsimd", lambda e, fc0=fc0: e.dma_start(
                [wfcs0, wfcs1][fc0][:, :], wfc_h.ap()[fc0 * 128:(fc0 + 1) * 128, :]),
                inc=dWFb[fc0])

        if DEBUG:
            OP("sync", lambda e: e.dma_start(dbg["dosbr"].ap(), osb[:, :]),
               waits=[(cS, cS.v), (dOSTb[0], dOSTb[0].v), (dOSTb[1], dOSTb[1].v),
                      (cV, cV.v)], inc=dM)
        recip_done[1] = OP("vector", lambda e: e.reciprocal(
            P["Rbuf"][32:64, :], P["Lbuf"][32:64, :]),
            waits=[(dLb[0], lrec_dma[1][0]), (dLb[1], lrec_dma[1][1])], inc=cV)
        ex = [(dOSTb[0], dOSTb[0].v), (dOSTb[1], dOSTb[1].v), (cV, cV.v)]
        for c_ in range(6, 8):
            emit_norm_chunk(c_, [rrep2, rrep3][c_ % 2], [osb22, osb23][c_ % 2],
                            osb, recip_done[1],
                            extra_first=ex, writeback=False)
        wpj_src = wpj_h.ap().rearrange("(a b) e -> b a e", a=32)
        wpj_dst = tap(wpj_sb, 0, 128, 0, [[1024, 32], [1, 1024]])
        wpj_ld = OP("gpsimd", lambda e, wpj_src=wpj_src, wpj_dst=wpj_dst: e.dma_start(
            wpj_dst, wpj_src), waits=[(cV, norm_state["mult_chunk_7"])], inc=dG)
        if DEBUG:
            OP("sync", lambda e: e.dma_start(dbg["dcosb0"].ap(), G["cosb"][:, :]),
               waits=[(cV, norm_state["mult_chunk_4"]), (dM, dM.v)], inc=dM)
        evacW = {0: None, 1: None}
        ast_dma = {}
        for tc in range(NTC):
            bank = tc % 2
            mmw = [(dG, wot_ld), (cS, evacW[bank])]
            if tc == 0:
                mmw += [(cS, cs_drain), (cV, cv_drain)]
            last = None
            for s in range(NCH):
                sw = list(mmw) if (s == 0) else []
                if s < 4:
                    sw.append((dWBb[s % 2], norm_state["wb_chunk_%d" % s]))
                else:
                    sw.append((cV, norm_state["mult_chunk_%d" % s]))
                for nh in range(2):
                    if s < 4:
                        src_t, src_off = osb, s * T
                    else:
                        src_t = {4: osb24, 5: osb25, 6: osb22, 7: osb23}[s]
                        src_off = 0
                    last = OP("tensor", lambda e, tc=tc, s=s, nh=nh, bank=bank,
                              src_t=src_t, src_off=src_off: e.matmul(
                        psW[bank][:, nh * 512:(nh + 1) * 512],
                        src_t[:, src_off + tc * 128: src_off + (tc + 1) * 128],
                        wot[:, s * D + nh * 512: s * D + (nh + 1) * 512],
                        start=(s == 0), stop=False),
                        waits=(sw if nh == 0 else []))
            for nh in range(2):
                last = OP("tensor", lambda e, nh=nh, bank=bank: e.matmul(
                    psW[bank][:, nh * 512:(nh + 1) * 512],
                    P["onesr"][0:1, 0:128], P["bob"][0:1, nh * 512:(nh + 1) * 512],
                    start=False, stop=True), inc=(cT if nh == 1 else None))
            aw = [(cT, last)]
            if tc >= 6:
                aw.append((dASb[tc % 6], ast_dma[tc - 6]))
            evacW[bank] = OP("scalar", lambda e, bank=bank, tc=tc: e.activation(
                asts[tc % 6][:, :], psW[bank][:, :], AF.Copy), waits=aw, inc=cS)
            ast_dma[tc] = OP("sync", lambda e, tc=tc: e.dma_start(
                ascr_h.ap()[tc * 128:(tc + 1) * 128, :], asts[tc % 6][:, :]),
                waits=[(cS, evacW[bank])], inc=dASb[tc % 6])
        ast_total = [c.v for c in dASb]
        evac_even_last = evacW[0]
        if DEBUG:
            OP("sync", lambda e: e.dma_start(dbg["dcosb1"].ap(), G["cosb"][:, :]),
               waits=[(cS, evacW[1]), (dM, dM.v)], inc=dM)
        rec.flush(nc)
    cs_p2d = cS.v

    for cm_ in reversed(ast_cms):
        cm_.__exit__(None, None, None)
    osb23_cm.__exit__(None, None, None)
    osb22_cm.__exit__(None, None, None)
    rrep3_cm.__exit__(None, None, None)
    rrep2_cm.__exit__(None, None, None)
    wot_cm.__exit__(None, None, None)
    osb_cm.__exit__(None, None, None)
    osb25_cm.__exit__(None, None, None)
    osb24_cm.__exit__(None, None, None)

    p3_named = {}
    p3_cms = []
    for name, shape, dt_ in [
        ("hr0", [128, 512], BF16), ("hr1", [128, 512], BF16),
        ("u01", [128, 1024], F32),
        ("u2_sb0", [128, 512], F32), ("u2_sb1", [128, 512], F32),
        ("yb01", [128, 1024], F32),
        ("s01", [128, 1024], BF16),
        ("acc01", [128, 1024], BF16), ("acc2_01", [128, 1024], BF16),
        ("hsb", [128, 32 * SUPT], BF16),
    ]:
        cm, t = alloc(name, shape, dt_)
        p3_cms.append(cm)
        p3_named[name] = t
    Q = p3_named
    wfcs = [wfcs0, wfcs1]
    hrs = [Q["hr0"], Q["hr1"]]
    xnTf = xnT.bitcast(F32)
    US2 = [xnTf[:, s * (T // 2): s * (T // 2) + 512] for s in range(4)]
    BS2 = [xnT[:, s * T + h * 512: s * T + (h + 1) * 512]
           for s in (4, 5, 6) for h in range(2)]
    PU = [None, bass.AP(xnTf, 0, [[8192, 128], [1024, 2], [1, 512]])]
    PYB = [None, bass.AP(xnTf, 2048, [[8192, 128], [1024, 2], [1, 512]])]
    PS = [None, xnT[:, 4 * T: 4 * T + 1024]]
    PACC = [None, [xnT[:, 5 * T: 5 * T + 1024], xnT[:, 6 * T: 6 * T + 1024]]]
    xt2s = [xts[0][:, 0:512], xts[0][:, 512:1024]]
    outsts = [xts[1][:, 0:512], xts[1][:, 512:1024]]
    at2s = [xns[0][:, 0:512], xns[0][:, 512:1024]]
    hsb = Q["hsb"]

    with nc.psum_tensor("psF0", [128, 512], F32) as psF0, \
         nc.psum_tensor("psF1", [128, 512], F32) as psF1, \
         nc.psum_tensor("psJ0", [128, 512], F32) as psJ0, \
         nc.psum_tensor("psJ1", [128, 512], F32) as psJ1, \
         nc.psum_tensor("psJ2", [128, 512], F32) as psJ2, \
         nc.psum_tensor("psJ3", [128, 512], F32) as psJ3, \
         nc.psum_tensor("psJ4", [128, 512], F32) as psJ4, \
         nc.psum_tensor("psJ5", [128, 512], F32) as psJ5:
        psF = [psF0, psF1]
        psJ = [psJ0, psJ1, psJ2, psJ3, psJ4, psJ5]
        psF_free = [None, None]
        ngroup = 0
        wfc_mm = {}
        pending_hsq = None
        hr_free = {}
        psj_free = [None] * 6
        xt2_free = {0: None, 1: None}
        at2_free = {0: None, 1: None}
        outst_free = {0: None, 1: None}
        u_free = [None, None, None, None]
        ntile = 0
        n_out = 0
        tile_loads = {}

        def issue_tile_load(su, eh, tg, tl):
            nonlocal tile_loads
            tb = (su * 16 + eh * 8 + tg * 4 + tl) % 2
            rows = su * SUPT + tg * 512 + tl * 128
            cols = eh * 512
            w1 = [(cV, xt2_free[tb])]
            if xt2_free[tb] is None:
                w1 += [(dLb[0], lrec_dma[1][0]), (dLb[1], lrec_dma[1][1])]
            l1 = OP("sync", lambda e, tb=tb, rows=rows, cols=cols: e.dma_start(
                xt2s[tb], x_h.ap()[rows:rows + 128, cols:cols + 512]),
                waits=w1, inc=dXb[tb])
            l2 = OP("sync", lambda e, tb=tb, rows=rows, cols=cols: e.dma_start(
                at2s[tb], ascr_h.ap()[rows:rows + 128, cols:cols + 512]),
                waits=[(dASb[i_], ast_total[i_]) for i_ in range(6)],
                inc=dATb[tb])
            tile_loads[(su, eh, tg, tl)] = (tb, l1, l2)

        for su in range(NSUP):
            for fc in range(32):
                fb = fc % 2
                if su == 0 and fc < 2:
                    wdma = wfc_pre[fc]
                else:
                    wwaits = [(cT, wfc_mm[(su * 32 + fc - 2)])]
                    wdma = OP("gpsimd", lambda e, fc=fc, fb=fb: e.dma_start(
                        wfcs[fb][:, :], wfc_h.ap()[fc * 128:(fc + 1) * 128, :]),
                        waits=wwaits, inc=dWFb[fb])
                for th in range(2):
                    pb = (fc * 2 + th) % 2
                    mmw = [(dWFb[fb], wdma), (cS, psF_free[pb])]
                    if su == 0 and fc == 0:
                        mmw.append((cS, evac_even_last))
                    last = None
                    for s in range(NCH):
                        last = OP("tensor", lambda e, su=su, fc=fc, th=th, s=s, pb=pb, fb=fb: e.matmul(
                            psF[pb][:, 0:512],
                            wfcs[fb][:, s * 128:(s + 1) * 128],
                            xnT[:, s * T + su * SUPT + th * 512: s * T + su * SUPT + (th + 1) * 512],
                            start=(s == 0), stop=(s == NCH - 1)),
                            waits=(mmw if s == 0 else []),
                            inc=(cT if s == NCH - 1 else None))
                    wfc_mm[su * 32 + fc] = last
                    hb = (fc * 2 + th) % 2
                    rw = [(cT, last), (cV, hr_free.get(hb))]
                    hrel = OP("scalar", lambda e, pb=pb, hb=hb: e.activation(
                        hrs[hb][:, :], psF[pb][:, 0:512], AF.Relu),
                        waits=rw, inc=cS)
                    psF_free[pb] = hrel
                    if pending_hsq is not None:
                        pending_hsq()

                    def mk_hsq(fc=fc, th=th, hb=hb, hrel=hrel, su=su):
                        def go():
                            hw = [(cS, hrel)]
                            if su == 0 and 16 <= fc <= 31:
                                hw += [(dASb[i_], ast_total[i_]) for i_ in range(6)]
                            hsq_v = OP("vector", lambda e, fc=fc, th=th, hb=hb: e.tensor_tensor(
                                hsb[:, fc * SUPT + th * 512: fc * SUPT + (th + 1) * 512],
                                hrs[hb][:, :], hrs[hb][:, :], MULT), waits=hw, inc=cV)
                            hr_free[hb] = hsq_v
                        return go
                    pending_hsq = mk_hsq()
            if pending_hsq is not None:
                pending_hsq()
                pending_hsq = None
            h_ready = cV.v
            for eh in range(2):
                for tg in range(2):
                    for tl in range(2):
                        issue_tile_load(su, eh, tg, tl)
                    slots = [(ngroup * 4 + tl) % 6 for tl in range(4)]
                    ngroup += 1
                    pj_last = [None] * 4
                    for fc in range(32):
                        for tl in range(4):
                            sl = slots[tl]
                            mmw = [(cV, h_ready), (dG, wpj_ld)]
                            if fc == 0:
                                mmw.append((cS, psj_free[sl]))
                            pj_last[tl] = OP("tensor", lambda e, su=su, eh=eh, tg=tg,
                                             fc=fc, tl=tl, sl=sl: e.matmul(
                                psJ[sl][:, 0:512],
                                hsb[:, fc * SUPT + (tg * 4 + tl) * 128: fc * SUPT + (tg * 4 + tl) * 128 + 128],
                                wpj_sb[:, fc * D + eh * 512: fc * D + (eh + 1) * 512],
                                start=(fc == 0), stop=(fc == 31)),
                                waits=(mmw if (fc == 0 or tl == 0) else []),
                                inc=(cT if fc == 31 else None))
                    US = [Q["u01"][:, 0:512], Q["u01"][:, 512:1024], US2[0], US2[1]]
                    U2S = [Q["u2_sb0"], Q["u2_sb1"]]
                    YBS = [Q["yb01"][:, 0:512], Q["yb01"][:, 512:1024],
                           US2[2], US2[3]]
                    ACCS = [[Q["acc01"][:, 0:512], Q["acc2_01"][:, 0:512]],
                            [Q["acc01"][:, 512:1024], Q["acc2_01"][:, 512:1024]],
                            [BS2[2], BS2[4]],
                            [BS2[3], BS2[5]]]
                    PU[0] = Q["u01"][:, :]
                    PYB[0] = Q["yb01"][:, :]
                    PS[0] = Q["s01"][:, :]
                    PACC[0] = [Q["acc01"][:, :], Q["acc2_01"][:, :]]
                    for pair in range(2):
                        tls = (2 * pair, 2 * pair + 1)
                        yev = {}
                        for tl in tls:
                            ntile += 1
                            sl = slots[tl]
                            uw = [(cT, pj_last[tl]), (cV, u_free[tl])]
                            OP("scalar", lambda e, sl=sl, tl=tl: e.activation(
                                US[tl][:, :], psJ[sl][:, 0:512], AF.Identity, bias=-1.0),
                                waits=uw)
                            psj_free[sl] = OP("scalar", lambda e, sl=sl, tl=tl: e.activation(
                                YBS[tl][:, :], psJ[sl][:, 0:512], AF.Identity, bias=-0.1),
                                inc=cS)
                            yev[tl] = psj_free[sl]
                        OP("vector", lambda e, pair=pair: e.tensor_single_scalar(
                            PS[pair], PU[pair], 0.0, ISGT),
                            waits=[(cS, yev[tls[0]]), (cS, yev[tls[1]])])
                        OP("vector", lambda e, pair=pair: e.tensor_copy(
                            PACC[pair][0], PS[pair]))
                        ai = 0
                        for _step in range(4):
                            OP("vector", lambda e, pair=pair: e.scalar_tensor_tensor(
                                PU[pair], PU[pair], 0.9, PYB[pair], MULT, ADD))
                            OP("vector", lambda e, pair=pair: e.tensor_tensor(
                                PU[pair], PU[pair], PS[pair], SUB))
                            OP("vector", lambda e, pair=pair: e.tensor_single_scalar(
                                PS[pair], PU[pair], 0.0, ISGT))
                            OP("vector", lambda e, pair=pair, ai=ai: e.tensor_tensor(
                                PACC[pair][1 - ai], PACC[pair][ai], PS[pair], ADD))
                            ai = 1 - ai
                        einfo = {}
                        for tl in tls:
                            tb, l1, l2 = tile_loads[(su, eh, tg, tl)]
                            ob = n_out % 2
                            n_out += 1
                            einfo[tl] = (tb, ob)
                            ew = [(dXb[tb], l1), (dATb[tb], l2),
                                  (dOUb[ob], outst_free[ob])]
                            OP("vector", lambda e, tb=tb, tl=tl, ai=ai: e.scalar_tensor_tensor(
                                U2S[tl % 2][:, :], ACCS[tl][ai][:, :], P["fifth"][:, 0:1],
                                at2s[tb], MULT, ADD), waits=ew)
                        for tl in tls:
                            tb, ob = einfo[tl]
                            vout = OP("vector", lambda e, tb=tb, ob=ob, tl=tl: e.tensor_tensor(
                                outsts[ob], U2S[tl % 2][:, :], xt2s[tb], ADD),
                                inc=cV)
                            xt2_free[tb] = vout
                            at2_free[tb] = vout
                            u_free[tl] = vout
                            rows = su * SUPT + tg * 512 + tl * 128
                            cols = eh * 512
                            outst_free[ob] = OP("sync", lambda e, ob=ob, rows=rows,
                                                cols=cols: e.dma_start(
                                out_h.ap()[rows:rows + 128, cols:cols + 512],
                                outsts[ob]), waits=[(cV, vout)], inc=dOUb[ob])
                        if pair == 0:
                            issue_tile_load(su, eh, tg, 2)
                            issue_tile_load(su, eh, tg, 3)
        if DEBUG:
            OP("sync", lambda e: e.dma_start(dbg["dhsb"].ap(), hsb[:, :]),
               waits=[(cV, cV.v)], inc=dM)
        OP("sync", lambda e: None,
           waits=[(dOUb[0], dOUb[0].v), (dOUb[1], dOUb[1].v)])
        rec.flush(nc)

    return nc


_CACHED = {}


def _host_prep(L0_attn_WQ_w, L0_attn_WQ_b, L0_attn_WK_w, L0_attn_WK_b,
               L0_attn_WV_w, L0_attn_WV_b, L0_attn_Wo_w, L0_attn_Wo_b,
               L0_ffn_W_fc, L0_ffn_W_proj):
    f32 = np.float32
    WQ = np.asarray(L0_attn_WQ_w, f32)
    WK = np.asarray(L0_attn_WK_w, f32)
    WV = np.asarray(L0_attn_WV_w, f32)
    WO = np.asarray(L0_attn_Wo_w, f32)
    Wfc = np.asarray(L0_ffn_W_fc, f32)
    Wpj = np.asarray(L0_ffn_W_proj, f32)

    def qk_tile(W):
        t = W.reshape(8, 128, 8, 128).transpose(0, 3, 2, 1)
        return np.ascontiguousarray(t).reshape(D, D)

    def de_tile(W):
        t = W.reshape(D, 8, 128).transpose(2, 1, 0)
        return np.ascontiguousarray(t).reshape(128, 8 * D)

    wq = qk_tile(WQ)
    wk = qk_tile(WK)
    wv = de_tile(WV)
    wo = de_tile(WO)
    wfc = np.ascontiguousarray(
        Wfc.reshape(32, 128, 8, 128).transpose(0, 3, 2, 1)).reshape(FF, D)
    wpj = np.ascontiguousarray(
        Wpj.reshape(D, 32, 128).transpose(1, 2, 0)).reshape(FF, D)

    bq = np.ascontiguousarray(np.asarray(L0_attn_WQ_b, f32).reshape(8, 128).T)
    bk = np.ascontiguousarray(np.asarray(L0_attn_WK_b, f32).reshape(8, 128).T)
    bv = np.asarray(L0_attn_WV_b, f32).reshape(1, D)
    bo = np.asarray(L0_attn_Wo_b, f32).reshape(1, D)

    half = DK // 2
    i = np.arange(half, dtype=np.float64)
    freq = 1.0 / 10000.0 ** (2.0 * i / DK)
    tpos = np.arange(T, dtype=np.float64)
    ang = tpos[None, :] * freq[:, None]
    cos32 = np.cos(ang)
    sin32 = np.sin(ang)
    cosf = np.empty((128, T), np.float64)
    sinf = np.empty((128, T), np.float64)
    for blk in range(4):
        cosf[blk * 32:(blk + 1) * 32] = cos32
        sgn = -1.0 if blk % 2 == 0 else 1.0
        sinf[blk * 32:(blk + 1) * 32] = sgn * sin32
    cosf = cosf.astype(f32)
    sinf = sinf.astype(f32)

    cb = np.zeros((128, 4 * 128), bfloat16)
    cb[:, 0:128] = np.eye(128, dtype=f32).astype(bfloat16)
    sp = np.zeros((128, 128), f32)
    for p in range(128):
        base = (p // 64) * 64
        d = p % 64
        sp[base + (d + 32) % 64, p] = 1.0
    cb[:, 128:256] = sp.astype(bfloat16)
    tri = np.where(np.arange(128)[None, :] < np.arange(128)[:, None],
                   f32(-30000.0), f32(0.0))
    cb[:, 256:384] = tri.astype(bfloat16)
    om = np.zeros((128, 128), f32)
    for c in range(8):
        om[0:64, c * 16 + 2 * c] = 1.0
        om[64:128, c * 16 + 2 * c + 1] = 1.0
    cb[:, 384:512] = om.astype(bfloat16)
    onesrow = np.ones((1, 128), bfloat16)

    return dict(wq=wq, wk=wk, wv=wv, wo=wo, wfc=wfc, wpj=wpj,
                bqv=bq, bkv=bk, bvv=bv, bov=bo, cosf=cosf, sinf=sinf,
                cb=cb, onesrow=onesrow,
                cvals_in=np.broadcast_to(
                    np.array([1e-5, 2e-5, -1.0, -0.1, 0.125, 0.0, 1.0, 5.0], f32),
                    (128, 8)).copy())


def kernel(**inputs):
    x = np.ascontiguousarray(np.asarray(inputs["L0_attn_x"], np.float32))
    shared = _host_prep(
        inputs["L0_attn_WQ_w"], inputs["L0_attn_WQ_b"],
        inputs["L0_attn_WK_w"], inputs["L0_attn_WK_b"],
        inputs["L0_attn_WV_w"], inputs["L0_attn_WV_b"],
        inputs["L0_attn_Wo_w"], inputs["L0_attn_Wo_b"],
        inputs["L0_ffn_W_fc"], inputs["L0_ffn_W_proj"])
    nc = build_nc()
    in_maps = []
    for i in range(8):
        m = dict(shared)
        m["x"] = np.ascontiguousarray(x[i])
        in_maps.append(m)
    res = run_bass_kernel_spmd(nc, in_maps, core_ids=list(range(8)))
    return np.stack([np.asarray(res.results[i]["out"]) for i in range(8)], axis=0)


if __name__ == "__main__":
    nc = build_nc()
    print("built:", nc.n_instructions(), "instructions")
